# version 02 (final)
# baseline (speedup 1.0000x reference)
import numpy as np
from contextlib import ExitStack
import concourse.bass as bass
import concourse.mybir as mybir
from concourse.bass_utils import run_bass_kernel_spmd

F32 = mybir.dt.float32
BF16 = mybir.dt.bfloat16
I32 = mybir.dt.int32
AF = mybir.ActivationFunctionType
ALU = mybir.AluOpType
AX = mybir.AxisListType.X

NC = 8
D = 2048
T = 16384
TC = T // NC
S = 8192
H = 16
HD = 128
NCH = D // 128
HID = 6144
LN_EPS = 1e-5
ISQ_HD = float(1.0 / np.sqrt(128.0))



def consts_setup(nc, E):
    import ml_dtypes
    c = {}
    cd = {}
    mmsk = (np.arange(896)[None, :] >= (np.arange(128)[:, None] + 384)).astype(np.float32)
    vals = {
        "ones_col": (np.ones((128, 1), np.float32), BF16),
        "ones_row": (np.ones((1, 128), np.float32), BF16),
        "ones_col_f": (np.ones((128, 1), np.float32), F32),
        "ones_row_f": (np.ones((1, 128), np.float32), F32),
        "one": (np.ones((1, 1), np.float32), BF16),
        "one_f": (np.ones((1, 1), np.float32), F32),
        "isq_hd": (np.full((128, 1), ISQ_HD, np.float32), F32),
        "neg_half": (np.full((128, 1), -0.5, np.float32), F32),
        "neg_one": (np.full((128, 1), -1.0, np.float32), F32),
        "cmask": (mmsk, BF16),
        "ident": (np.eye(128, dtype=np.float32), BF16),
        "ident_f": (np.eye(128, dtype=np.float32), F32),
    }
    for k, (arr, dt) in vals.items():
        a = arr.astype(ml_dtypes.bfloat16) if dt == BF16 else arr.astype(np.float32)
        cd[k] = nc.inline_tensor(a, name="c_" + k)
        c[k] = E(nc.sbuf_tensor("cst_" + k, list(arr.shape), dt))
    c["_dram"] = cd
    c["_src"] = {k: k for k in vals}
    return c


def consts_load(g, c, sem):
    n = 0
    for k, d_k in c["_src"].items():
        g.dma_start(out=c[k][:, :], in_=c["_dram"][d_k].ap()).then_inc(sem, 16)
        n += 16
    return n


CONST_N = 12 * 16




def build_p1():
    nc = bass.Bass("TRN2", target_bir_lowering=False)
    x = nc.dram_tensor("x_own", [TC, D], F32, kind="ExternalInput").ap()
    st = nc.dram_tensor("st", [2, TC], F32, kind="ExternalOutput").ap()
    NT = TC // 128
    SERIAL_DEBUG = True
    with ExitStack() as ctx:
        E = ctx.enter_context
        c = consts_setup(nc, E)
        xt = E(nc.sbuf_tensor("xt", [128, 16 * D], F32))
        sqd = E(nc.sbuf_tensor("sqd", [128, D], F32))
        s1 = E(nc.sbuf_tensor("s1", [128, 16], F32))
        s2 = E(nc.sbuf_tensor("s2", [128, 16], F32))
        m_ = E(nc.sbuf_tensor("m_", [128, 16], F32))
        var = E(nc.sbuf_tensor("var", [128, 16], F32))
        r_ = E(nc.sbuf_tensor("r_", [128, 16], F32))
        nm = E(nc.sbuf_tensor("nm", [128, 16], F32))
        rowo = E(nc.sbuf_tensor("rowo", [16, 256], F32))

        ps = E(nc.psum_tensor("ps", [16, 256], F32))
        ld = E(nc.semaphore("ld"))
        vv = E(nc.semaphore("vv"))
        ss = E(nc.semaphore("ss"))
        mm = E(nc.semaphore("mm"))
        ee = E(nc.semaphore("ee"))
        stm = E(nc.semaphore("stm"))
        block = E(nc.Block())

        @block.gpsimd
        def _(g):
            consts_load(g, c, ld)
            for i in range(NT):
                g.dma_start(out=xt[:, i * D:(i + 1) * D],
                            in_=x[i * 128:(i + 1) * 128, :]).then_inc(ld, 16)

        @block.vector
        def _(v):
            for i in range(NT):
                v.wait_ge(ld, CONST_N + 16 * (NT if SERIAL_DEBUG else (i + 1)))
                xv = xt[:, i * D:(i + 1) * D]
                v.reduce_sum(s1[:, i:i + 1], xv, axis=AX).then_inc(vv, 1)
            v.drain()
            v.wait_ge(ss, NT + 1)
            v.tensor_single_scalar(m_[:, :], s1[:, :], 1.0 / D, op=ALU.mult)
            v.tensor_single_scalar(var[:, :], s2[:, :], 1.0 / D, op=ALU.mult)
            v.drain()
            v.tensor_copy(s1[:, :], m_[:, :])
            v.drain()
            v.tensor_tensor(s1[:, :], s1[:, :], m_[:, :], op=ALU.mult)
            v.drain()
            v.tensor_sub(var[:, :], var[:, :], s1[:, :])
            v.drain()
            v.tensor_single_scalar(var[:, :], var[:, :], LN_EPS, op=ALU.add)
            v.drain()
            v.tensor_single_scalar(nm[:, :], m_[:, :], -1.0, op=ALU.mult)
            v.drain()
            v.drain()
            v.nop().then_inc(vv, 1)

        @block.scalar
        def _(sc):
            for i in range(NT):
                sc.wait_ge(ld, CONST_N + 16 * (NT if SERIAL_DEBUG else (i + 1)))
                xv = xt[:, i * D:(i + 1) * D]
                sc.activation(sqd[:, 0:D], xv, AF.Square, accum_out=s2[:, i:i + 1]).then_inc(ss, 1)
            sc.drain()
            sc.nop().then_inc(ss, 1)
            sc.wait_ge(vv, NT + 1)
            sc.activation(sqd[:, 0:16], var[:, :], AF.Ln)
            sc.activation(r_[:, :], sqd[:, 0:16], AF.Exp, scale=c["neg_half"][:, :1])
            sc.drain()
            sc.nop().then_inc(ss, 1)
            sc.wait_ge(mm, 2)
            sc.copy(rowo[:, 0:128], ps[:, 0:128])
            sc.copy(rowo[:, 128:256], ps[:, 128:256])
            sc.drain()
            sc.nop().then_inc(ee, 1)

        @block.tensor
        def _(t):
            t.wait_ge(ld, CONST_N)
            t.wait_ge(vv, NT + 1)
            t.wait_ge(ss, NT + 2)
            t.transpose(ps[:, 0:128], r_[:, :], c["ident_f"][:, :]).then_inc(mm, 1)
            t.transpose(ps[:, 128:256], nm[:, :], c["ident_f"][:, :]).then_inc(mm, 1)

        @block.sync
        def _(sy):
            sy.wait_ge(ee, 1)
            sy.dma_start(out=st[0:1, :].rearrange("a (t p) -> (a t) p", p=128),
                         in_=rowo[0:16, 0:128]).then_inc(stm, 16)
            sy.dma_start(out=st[1:2, :].rearrange("a (t p) -> (a t) p", p=128),
                         in_=rowo[0:16, 128:256]).then_inc(stm, 16)
            sy.wait_ge(stm, 32)
    return nc


def _run(nc, in_maps):
    from concourse.bass_utils import run_bass_kernel_spmd
    return run_bass_kernel_spmd(nc, in_maps, core_ids=list(range(NC))).results


def kernel(**inputs):
    import ml_dtypes
    bfd = ml_dtypes.bfloat16
    cat = np.concatenate
    ac = np.ascontiguousarray
    own = [slice(TC * c, TC * (c + 1)) for c in range(NC)]
    x = np.asarray(inputs["L0_attn_x"], np.float32).reshape(T, D)
    xT = ac(x.T)

    res = _run(build_p1(), [{"x_own": ac(x[own[c]])} for c in range(NC)])
    stT = ac(cat([np.asarray(r["st"], np.float32) for r in res], 1))
    stc = ac(stT[0].reshape(128, 128).T)

    im = []
    for c in range(NC):
        sl = slice(256 * c, 256 * (c + 1))
        im.append({"xT": xT, "stT": stT, "stc": stc,
                   "wq_s": ac(inputs["L0_attn_wq"][:, sl]),
                   "wk_s": ac(inputs["L0_attn_wk"][:, sl]),
                   "wv_s": ac(inputs["L0_attn_wv"][:, sl]),
                   "bq_s": ac(inputs["L0_attn_bq"][sl][None, :]),
                   "bk_s": ac(inputs["L0_attn_bk"][sl][None, :]),
                   "bv_s": ac(inputs["L0_attn_bv"][sl][None, :])})
    res = _run(build_p2(), im)
    ctxg = cat([np.asarray(r["ctx"], bfd) for r in res], 0)

    bp = ac(inputs["L0_attn_bp"][None, :].astype(np.float32))
    im = [{"ctxg": ac(ctxg[:, own[c]]), "wp": inputs["L0_attn_wp"], "bp": bp,
           "xTo": ac(xT[:, own[c]])} for c in range(NC)]
    res = _run(build_p3(), im)
    x1T = cat([np.asarray(r["x1o"], np.float32) for r in res], 1)
    xn1T = cat([np.asarray(r["xn1o"], bfd) for r in res], 1)

    w1, w2 = inputs["L0_ffn_w1"], inputs["L0_ffn_w2"]
    shards = [{"w1s": ac(w1[1024 * c:1024 * (c + 1)]),
               "w2s": ac(w2[256 * c:256 * (c + 1)])} for c in range(NC)]
    res = _run(build_p4a(), shards)
    totals = ac(cat([np.asarray(r["po"], np.float32) for r in res], 0))
    im = [dict(shards[c], totals=totals) for c in range(NC)]
    res = _run(build_p4b(), im)
    w1tg = cat([np.asarray(r["w1tb"], bfd) for r in res], 0)
    w2tg = cat([np.asarray(r["w2tb"], bfd) for r in res], 0)

    b1c = ac(inputs["L0_ffn_b1"].reshape(64, 128).T.astype(np.float32))
    b2c = ac(inputs["L0_ffn_b2"].reshape(16, 128).T.astype(np.float32))
    im = [{"xng": ac(xn1T[:, own[c]]), "w1tg": w1tg, "w2tg": w2tg,
           "b1c": b1c, "b2c": b2c} for c in range(NC)]
    res = _run(build_p5a(), im)
    f5 = [np.asarray(r["fout"], bfd) for r in res]

    im = [{"ctxg": ac(f5[c]), "xTo": ac(x1T[:, own[c]])} for c in range(NC)]
    res = _run(build_tail(), im)
    x2T = cat([np.asarray(r["x1o"], np.float32) for r in res], 1)
    zn2T = cat([np.asarray(r["xn1o"], bfd) for r in res], 1)

    ZP = 128
    wg = np.full((HID, D), ZP, np.int32); wg[:5632] = inputs["L1_ffn_wq_gate"]
    wu = np.full((HID, D), ZP, np.int32); wu[:5632] = inputs["L1_ffn_wq_up"]
    wd = np.full((D, HID), ZP, np.int32); wd[:, :5632] = inputs["L1_ffn_wq_down"]
    im = [{"gshard": ac(wg[768 * c:768 * (c + 1)]),
           "ushard": ac(wu[768 * c:768 * (c + 1)]),
           "dshard": ac(wd[256 * c:256 * (c + 1)])} for c in range(NC)]
    res = _run(build_p6(), im)
    wgTg = cat([np.asarray(r["wgT"], bfd) for r in res], 0)
    wuTg = cat([np.asarray(r["wuT"], bfd) for r in res], 0)
    wdTg = cat([np.asarray(r["wdT"], bfd) for r in res], 0)

    taps = cat([np.asarray(inputs["L1_attn_conv_q"], np.float32),
                np.asarray(inputs["L1_attn_conv_k"], np.float32),
                np.asarray(inputs["L1_attn_conv_v"], np.float32)])
    tapc = ac(np.tile(taps[None, :], (128, 1)))
    onw = ac(np.asarray(inputs["L1_attn_onorm_w"], np.float32).reshape(1, 128))
    im = []
    for c in range(NC):
        sl = slice(256 * c, 256 * (c + 1))
        im.append({"zng": ac(zn2T), "tapc": tapc, "onw": onw,
                   "wq_s": ac(inputs["L1_attn_Wq"][:, sl].astype(np.float32)),
                   "wk_s": ac(inputs["L1_attn_Wk"][:, sl].astype(np.float32)),
                   "wv_s": ac(inputs["L1_attn_Wv"][:, sl].astype(np.float32)),
                   "wb_s": ac(inputs["L1_attn_Wb"][:, 2 * c:2 * c + 2].astype(np.float32))})
    res = _run(build_p7(), im)
    onT = np.empty((D, T), bfd)
    for c in range(NC):
        o = np.asarray(res[c]["ofm"], bfd)
        onT[(2 * c) * 128:(2 * c + 1) * 128] = o[0:128]
        onT[(2 * c + 1) * 128:(2 * c + 2) * 128] = o[128:256]

    im = [{"ctxg": ac(onT[:, own[c]]), "wp": inputs["L1_attn_Wo"],
           "xTo": ac(x2T[:, own[c]])} for c in range(NC)]
    res = _run(build_p8(), im)
    x3T = cat([np.asarray(r["x1o"], np.float32) for r in res], 1)
    xn3T = cat([np.asarray(r["xn1o"], bfd) for r in res], 1)

    sg = np.zeros(HID, np.float32); sg[:5632] = inputs["L1_ffn_s_gate"]
    su = np.zeros(HID, np.float32); su[:5632] = inputs["L1_ffn_s_up"]
    sgc = ac(sg.reshape(48, 128).T)
    suc = ac(su.reshape(48, 128).T)
    sdc = ac(inputs["L1_ffn_s_down"].reshape(16, 128).T.astype(np.float32))
    im = [{"xng": ac(xn3T[:, own[c]]), "wgTg": wgTg, "wuTg": wuTg, "wdTg": wdTg,
           "sgc": sgc, "suc": suc, "sdc": sdc} for c in range(NC)]
    res = _run(build_p9(), im)
    f9 = [np.asarray(r["fout"], bfd) for r in res]

    im = [{"ctxg": ac(f9[c]), "xTo": ac(x3T[:, own[c]])} for c in range(NC)]
    res = _run(build_tail(), im)
    x4T = cat([np.asarray(r["x1o"], np.float32) for r in res], 1)
    global _dbg
    _dbg = {"ctxg": ctxg, "x1T": x1T, "xn1T": xn1T, "f5": f5, "x2T": x2T,
            "zn2T": zn2T, "onT": onT, "x3T": x3T, "xn3T": xn3T, "f9": f9}
    return ac(x4T.T).reshape(2, 8192, 2048)



class _Ev:
    __slots__ = ("eng", "idx")

    def __init__(self, eng, idx):
        self.eng = eng
        self.idx = idx


NDSEM = 6


class Sched:
    ENGS = ("t", "v", "s", "g", "y")

    def __init__(self, nc, E, tag=""):
        self.nc = nc
        self.steps = {e: [] for e in self.ENGS}
        self.sems = {e: E(nc.semaphore(f"sch{tag}_{e}")) for e in self.ENGS}
        self.cnt = {e: 0 for e in self.ENGS}
        for e in ("g", "y", "s"):
            for k in range(NDSEM):
                key = f"{e}d{k}"
                self.sems[key] = E(nc.semaphore(f"sch{tag}_{key}"))
                self.cnt[key] = 0
        self.dnext = {e: 0 for e in ("g", "y", "s")}

    def step(self, eng, emit, waits=(), signal=False, dma=False):
        assert not dma
        ev = None
        if signal:
            self.cnt[eng] += 2 if eng == "s" else 1
            ev = _Ev(eng, self.cnt[eng])
        self.steps[eng].append(("step", emit, [w for w in waits if w is not None], ev, None))
        return ev

    def dma_group(self, eng, emits, waits=()):
        key = f"{eng}d{self.dnext[eng]}"
        self.dnext[eng] = (self.dnext[eng] + 1) % NDSEM
        prev_total = self.cnt[key]
        self.cnt[key] += 16 * len(emits)
        ev = _Ev(key, self.cnt[key])
        self.steps[eng].append(("dma", emits, [w for w in waits if w is not None], ev, prev_total))
        return ev

    def dma(self, eng, emit, waits=()):
        return self.dma_group(eng, [emit], waits)

    def check(self):
        pos = {e: 0 for e in self.ENGS}
        done = {k: 0 for k in self.cnt}
        progress = True
        while progress:
            progress = False
            for e in self.ENGS:
                while pos[e] < len(self.steps[e]):
                    kind, emit, waits, ev, prev_total = self.steps[e][pos[e]]
                    if any(done[w.eng] < w.idx for w in waits):
                        break
                    if kind == "dma" and done[ev.eng] < prev_total:
                        break
                    if ev is not None:
                        done[ev.eng] = max(done[ev.eng], ev.idx)
                    pos[e] += 1
                    progress = True
        stuck = {e: (pos[e], len(self.steps[e])) for e in self.ENGS if pos[e] < len(self.steps[e])}
        if stuck:
            detail = {}
            for e in stuck:
                _, _, waits, ev, prev_total = self.steps[e][pos[e]]
                detail[e] = [(w.eng, w.idx, done[w.eng]) for w in waits]
            raise RuntimeError(f"Sched deadlock: stuck={stuck} waits(eng,need,have)={detail}")

    def run(self, block):
        self.check()

        def mk(eng_key):
            def _run(h):
                for kind, emit, waits, ev, prev_total in self.steps[eng_key]:
                    for w in waits:
                        h.wait_ge(self.sems[w.eng], w.idx)
                    if kind == "dma":
                        h.wait_ge(self.sems[ev.eng], prev_total)
                        for em in emit:
                            em(h).then_inc(self.sems[ev.eng], 16)
                    else:
                        r = emit(h)
                        if ev is not None:
                            if eng_key == "t":
                                assert r is not None and not isinstance(r, list), \
                                    "tensor step emitters must return their last matmul"
                                r.then_inc(self.sems[eng_key], 1)
                            elif eng_key == "s":
                                assert r is not None and not isinstance(r, list), \
                                    "scalar step emitters must return their last instruction"
                                r.then_inc(self.sems[eng_key], 1)
                                h.drain()
                                h.nop().then_inc(self.sems[eng_key], 1)
                            else:
                                h.drain()
                                h.nop().then_inc(self.sems[eng_key], 1)
            return _run

        block.tensor(mk("t"))
        block.vector(mk("v"))
        block.scalar(mk("s"))
        block.gpsimd(mk("g"))
        block.sync(mk("y"))




def build_p3():
    return _build_projtail(proj=True, with_bias=True)


def build_p8():
    return _build_projtail(proj=True, with_bias=False)


def build_tail():
    return _build_projtail(proj=False, with_bias=False)


def _build_projtail(proj=True, with_bias=True):
    nc = bass.Bass("TRN2", target_bir_lowering=False)
    ctxg = nc.dram_tensor("ctxg", [D, TC], BF16, kind="ExternalInput").ap()
    if proj:
        wp = nc.dram_tensor("wp", [D, D], F32, kind="ExternalInput").ap()
    if with_bias:
        bp = nc.dram_tensor("bp", [1, D], F32, kind="ExternalInput").ap()
    xTo = nc.dram_tensor("xTo", [D, TC], F32, kind="ExternalInput").ap()
    x1o = nc.dram_tensor("x1o", [D, TC], F32, kind="ExternalOutput").ap()
    xn1o = nc.dram_tensor("xn1o", [D, TC], BF16, kind="ExternalOutput").ap()

    NTW = TC // 512
    with ExitStack() as ctx:
        E = ctx.enter_context
        c = consts_setup(nc, E)
        wpt = E(nc.sbuf_tensor("wpt", [128, NCH * D], BF16)) if proj else None
        bpr = E(nc.sbuf_tensor("bpr", [1, D], F32))
        bpc = E(nc.sbuf_tensor("bpc", [128, NCH], F32))
        ctxb = E(nc.sbuf_tensor("ctxb", [128, 2 * NCH * 512], BF16))
        x0b = E(nc.sbuf_tensor("x0b", [128, NCH * 512], F32))
        fsb = E(nc.sbuf_tensor("fsb", [128, NCH * 512], BF16))
        sqb = E(nc.sbuf_tensor("sqb", [128, 2 * 512], BF16))
        xsqb = E(nc.sbuf_tensor("xsqb", [128, 2 * 512], BF16))
        x1sb = E(nc.sbuf_tensor("x1sb", [128, NCH * 512], F32))
        xnsb = E(nc.sbuf_tensor("xnsb", [128, 2 * 512], BF16))
        tmpa = E(nc.sbuf_tensor("tmpa", [128, 2 * 512], F32))
        tmpb = E(nc.sbuf_tensor("tmpb", [128, 2 * 512], F32))
        rows = E(nc.sbuf_tensor("rows", [1, 3 * 512], F32))
        ps = E(nc.psum_tensor("ps", [128, 4096], F32))
        block = E(nc.Block())
        sd = Sched(nc, E, tag="p3")

        PB = {"fsum": 1024, "fsq": 1536, "xsum": 2048, "xsq": 2560, "mbc": 3072, "rbc": 3584}

        ev_c = emit_consts(sd, c)
        ev_wp = None
        if proj:
            ev_wp = sd.dma_group("g", [
                (lambda g, ic=ic: g.dma_start(
                    out=wpt[:, ic * D:(ic + 1) * D], in_=wp[ic * 128:(ic + 1) * 128, :]))
                for ic in range(NCH)])
        if with_bias:
            ev_bp = sd.dma("s", lambda g: g.dma_start(out=bpr[:, :], in_=bp[:, :]))
            ev_bpcmm = sd.step("t", lambda t: [t.matmul(ps[:, 3072 + oc:3073 + oc],
                                                        bpr[0:1, oc * 128:(oc + 1) * 128],
                                                        c["one_f"][:, :], start=True, stop=True)
                                              for oc in range(NCH)][-1],
                               waits=[ev_c, ev_bp], signal=True)
            ev_bpc = sd.step("s", lambda s: s.copy(bpc[:, :], ps[:, 3072:3072 + NCH]),
                             waits=[ev_bpcmm], signal=True)
        else:
            ev_bpc = sd.step("v", lambda v: v.memset(bpc[:, :], 0.0), waits=[ev_c], signal=True)

        ev_ctx = [None] * NTW
        ev_x0 = [None] * NTW
        ev_proj = [[None] * NCH for _ in range(NTW)]
        ev_fevac = [[None] * NCH for _ in range(NTW)]
        ev_fstat = [[None] * NCH for _ in range(NTW)]
        ev_x1 = [[None] * NCH for _ in range(NTW)]
        ev_xsq = [[None] * NCH for _ in range(NTW)]
        ev_xstat = [[None] * NCH for _ in range(NTW)]
        ev_xn = [[None] * NCH for _ in range(NTW)]
        ev_frows = [None] * NTW
        ev_xrows = [None] * NTW
        ev_fbcmm = [None] * NTW
        ev_xbcmm = [None] * NTW
        ev_x1wr = [None] * NTW
        ev_xnwr = [None] * NTW
        ev_xn_wr_slot = [None] * NCH

        for tw in range(NTW):
            par = tw % 2
            ev_ctx[tw] = sd.dma_group("g", [
                (lambda g, ic=ic, tw=tw, par=par: g.dma_start(
                    out=ctxb[:, (par * NCH + ic) * 512:(par * NCH + ic) * 512 + 512],
                    in_=ctxg[ic * 128:(ic + 1) * 128, tw * 512:(tw + 1) * 512]))
                for ic in range(NCH)],
                waits=[(ev_proj[tw - 2][NCH - 1] if proj else ev_fevac[tw - 2][NCH - 1])
                       if tw >= 2 else None])
            ev_x0[tw] = sd.dma_group("g", [
                (lambda g, ic=ic, tw=tw: g.dma_start(
                    out=x0b[:, ic * 512:(ic + 1) * 512],
                    in_=xTo[ic * 128:(ic + 1) * 128, tw * 512:(tw + 1) * 512]))
                for ic in range(NCH)],
                waits=[ev_x1[tw - 1][NCH - 1] if tw >= 1 else None])
            for oc in range(NCH):
                if proj:
                    def _proj(t, tw=tw, par=par, oc=oc):
                        pb = (oc % 2) * 512
                        for ic in range(NCH):
                            r = t.matmul(ps[:, pb:pb + 512],
                                         wpt[:, ic * D + oc * 128:ic * D + oc * 128 + 128],
                                         ctxb[:, (par * NCH + ic) * 512:(par * NCH + ic) * 512 + 512],
                                         start=(ic == 0), stop=(ic == NCH - 1))
                        return r
                    prev_bank_user = ev_fevac[tw][oc - 2] if oc >= 2 else (
                        ev_fevac[tw - 1][NCH - 2 + oc] if tw >= 1 else None)
                    ev_proj[tw][oc] = sd.step("t", _proj,
                                              waits=[ev_ctx[tw], ev_bpc, ev_wp, prev_bank_user],
                                              signal=True)
                    ev_fev = sd.step("v", lambda v, tw=tw, oc=oc: v.tensor_scalar(
                        fsb[:, oc * 512:(oc + 1) * 512], ps[:, (oc % 2) * 512:(oc % 2) * 512 + 512],
                        bpc[:, oc:oc + 1], None, op0=ALU.add),
                        waits=[ev_proj[tw][oc], ev_bpc], signal=True)
                else:
                    ev_proj[tw][oc] = ev_ctx[tw]
                    ev_fev = sd.step("v", lambda v, tw=tw, par=par, oc=oc: v.tensor_copy(
                        fsb[:, oc * 512:(oc + 1) * 512],
                        ctxb[:, (par * NCH + oc) * 512:(par * NCH + oc) * 512 + 512]),
                        waits=[ev_ctx[tw], ev_fevac[tw - 2][NCH - 1] if tw >= 2 else None],
                        signal=True)
                ev_fevac[tw][oc] = sd.step("s", lambda s, tw=tw, oc=oc: s.activation(
                    sqb[:, (oc % 2) * 512:(oc % 2) * 512 + 512],
                    fsb[:, oc * 512:(oc + 1) * 512], AF.Square),
                    waits=[ev_fev, ev_fstat[tw][oc - 2] if oc >= 2 else None], signal=True)

                def _fstat(t, tw=tw, oc=oc):
                    t.matmul(ps[0:1, PB["fsum"]:PB["fsum"] + 512], c["ones_col"][:, :],
                             fsb[:, oc * 512:oc * 512 + 512], start=(oc == 0), stop=(oc == NCH - 1))
                    return t.matmul(ps[0:1, PB["fsq"]:PB["fsq"] + 512], c["ones_col"][:, :],
                                    sqb[:, (oc % 2) * 512:(oc % 2) * 512 + 512],
                                    start=(oc == 0), stop=(oc == NCH - 1))
                ev_fstat[tw][oc] = sd.step("t", _fstat,
                                           waits=[ev_fevac[tw][oc],
                                                  ev_xrows[tw - 1] if (oc == 0 and tw >= 1) else None],
                                           signal=True)
            ev_frv = sd.step("v", lambda v, tw=tw: _ln_rows(v, ps, rows, PB["fsum"], PB["fsq"]),
                             waits=[ev_fstat[tw][NCH - 1],
                                    ev_xbcmm[tw - 1] if tw >= 1 else None], signal=True)
            ev_frows[tw] = sd.step("s", lambda s, tw=tw: _ln_rows_fin(s, c, rows),
                                   waits=[ev_frv], signal=True)
            ev_fbcmm[tw] = sd.step("t", lambda t, tw=tw: [
                t.matmul(ps[:, PB["mbc"]:PB["mbc"] + 512], c["ones_row_f"][:, :],
                         rows[0:1, 0:512], start=True, stop=True),
                t.matmul(ps[:, PB["rbc"]:PB["rbc"] + 512], c["ones_row_f"][:, :],
                         rows[0:1, 2 * 512:2 * 512 + 512], start=True, stop=True)][-1],
                waits=[ev_frows[tw], ev_xn[tw - 1][NCH - 1] if tw >= 1 else ev_bpc], signal=True)
            for oc in range(NCH):
                def _x1ap(v, tw=tw, oc=oc):
                    sl = slice(oc * 512, (oc + 1) * 512)
                    s2 = slice((oc % 2) * 512, (oc % 2) * 512 + 512)
                    v.tensor_tensor(tmpa[:, s2], fsb[:, sl], ps[:, PB["mbc"]:PB["mbc"] + 512], op=ALU.subtract)
                    v.tensor_tensor(tmpb[:, s2], tmpa[:, s2], ps[:, PB["rbc"]:PB["rbc"] + 512], op=ALU.mult)
                    v.tensor_tensor(x1sb[:, sl], tmpb[:, s2], x0b[:, sl], op=ALU.add)
                ev_x1[tw][oc] = sd.step("v", _x1ap,
                                        waits=[ev_fbcmm[tw], ev_x0[tw], ev_fevac[tw][oc],
                                               ev_x1wr[tw - 1] if (oc == 0 and tw >= 1) else None],
                                        signal=True)
                ev_xsq[tw][oc] = sd.step("s", lambda s, tw=tw, oc=oc: s.activation(
                    xsqb[:, (oc % 2) * 512:(oc % 2) * 512 + 512],
                    x1sb[:, oc * 512:(oc + 1) * 512], AF.Square),
                    waits=[ev_x1[tw][oc], ev_xstat[tw][oc - 2] if oc >= 2 else None], signal=True)

                def _xstat(t, tw=tw, oc=oc):
                    t.matmul(ps[0:1, PB["xsum"]:PB["xsum"] + 512], c["ones_col_f"][:, :],
                             x1sb[:, oc * 512:oc * 512 + 512], start=(oc == 0), stop=(oc == NCH - 1))
                    return t.matmul(ps[0:1, PB["xsq"]:PB["xsq"] + 512], c["ones_col"][:, :],
                                    xsqb[:, (oc % 2) * 512:(oc % 2) * 512 + 512],
                                    start=(oc == 0), stop=(oc == NCH - 1))
                ev_xstat[tw][oc] = sd.step("t", _xstat, waits=[ev_xsq[tw][oc]], signal=True)
            ev_xrv = sd.step("v", lambda v, tw=tw: _ln_rows(v, ps, rows, PB["xsum"], PB["xsq"]),
                             waits=[ev_xstat[tw][NCH - 1], ev_fbcmm[tw]], signal=True)
            ev_xrows[tw] = sd.step("s", lambda s, tw=tw: _ln_rows_fin(s, c, rows),
                                   waits=[ev_xrv], signal=True)
            ev_xbcmm[tw] = sd.step("t", lambda t, tw=tw: [
                t.matmul(ps[:, PB["mbc"]:PB["mbc"] + 512], c["ones_row_f"][:, :],
                         rows[0:1, 0:512], start=True, stop=True),
                t.matmul(ps[:, PB["rbc"]:PB["rbc"] + 512], c["ones_row_f"][:, :],
                         rows[0:1, 2 * 512:2 * 512 + 512], start=True, stop=True)][-1],
                waits=[ev_xrows[tw], ev_x1[tw][NCH - 1]], signal=True)
            for oc in range(NCH):
                def _xnap(v, tw=tw, oc=oc):
                    sl = slice(oc * 512, (oc + 1) * 512)
                    s2 = slice((oc % 2) * 512, (oc % 2) * 512 + 512)
                    v.tensor_tensor(tmpa[:, s2], x1sb[:, sl], ps[:, PB["mbc"]:PB["mbc"] + 512], op=ALU.subtract)
                    v.tensor_tensor(xnsb[:, s2], tmpa[:, s2], ps[:, PB["rbc"]:PB["rbc"] + 512], op=ALU.mult)
                ev_xn[tw][oc] = sd.step("v", _xnap,
                                        waits=[ev_xbcmm[tw],
                                               ev_xn_wr_slot[oc - 2] if oc >= 2 else (
                                                   ev_xnwr[tw - 1] if tw >= 1 else None)],
                                        signal=True)
                ev_xn_wr_slot[oc] = sd.dma("y", lambda y, tw=tw, oc=oc: y.dma_start(
                    out=xn1o[oc * 128:(oc + 1) * 128, tw * 512:(tw + 1) * 512],
                    in_=xnsb[:, (oc % 2) * 512:(oc % 2) * 512 + 512]),
                    waits=[ev_xn[tw][oc]])
            ev_xnwr[tw] = ev_xn_wr_slot[NCH - 1]
            ev_x1wr[tw] = sd.dma_group("y", [
                (lambda y, tw=tw, oc=oc: y.dma_start(
                    out=x1o[oc * 128:(oc + 1) * 128, tw * 512:(tw + 1) * 512],
                    in_=x1sb[:, oc * 512:(oc + 1) * 512]))
                for oc in range(NCH)],
                waits=[ev_x1[tw][NCH - 1]])
        sd.step("y", lambda y: y.nop(), waits=[ev_x1wr[NTW - 1], ev_xnwr[NTW - 1]])
        sd.run(block)
    return nc


def emit_consts(sd, c, eng="s"):
    ev = None
    for k, d_k in c["_src"].items():
        ev = sd.dma(eng, lambda g, k=k, d_k=d_k: g.dma_start(out=c[k][:, :], in_=c["_dram"][d_k].ap()))
    return ev


def _ln_rows(v, ps, rows, off_sum, off_sq):
    r0 = rows[0:1, 0:512]
    r1 = rows[0:1, 512:1024]
    r2 = rows[0:1, 1024:1536]
    v.tensor_single_scalar(r0, ps[0:1, off_sum:off_sum + 512], 1.0 / D, op=ALU.mult)
    v.tensor_single_scalar(r1, ps[0:1, off_sq:off_sq + 512], 1.0 / D, op=ALU.mult)
    v.drain()
    v.tensor_copy(r2, r0)
    v.drain()
    v.tensor_tensor(r2, r2, r0, op=ALU.mult)
    v.drain()
    v.tensor_tensor(r2, r1, r2, op=ALU.subtract)
    v.drain()
    v.tensor_single_scalar(r2, r2, LN_EPS, op=ALU.add)


def _ln_rows_fin(s, c, rows):
    r2 = rows[0:1, 1024:1536]
    s.activation(r2, r2, AF.Ln)
    s.drain()
    return s.activation(r2, r2, AF.Exp, scale=c["neg_half"][0:1, 0:1])



def build_p4a():
    nc = bass.Bass("TRN2", target_bir_lowering=False)
    w1s = nc.dram_tensor("w1s", [1024, D], F32, kind="ExternalInput").ap()
    w2s = nc.dram_tensor("w2s", [256, 8192], F32, kind="ExternalInput").ap()
    po = nc.dram_tensor("po", [1, 2], F32, kind="ExternalOutput").ap()
    with ExitStack() as ctx:
        E = ctx.enter_context
        c = consts_setup(nc, E)
        buf = E(nc.sbuf_tensor("buf", [128, 2 * 4096], F32))
        acc = E(nc.sbuf_tensor("acc", [128, 16], F32))
        accs = E(nc.sbuf_tensor("accs", [128, 2], F32))
        out2 = E(nc.sbuf_tensor("out2", [2, 1], F32))
        ps = E(nc.psum_tensor("ps", [2, 1], F32))
        block = E(nc.Block())
        sd = Sched(nc, E, tag="p4a")
        ev_c = emit_consts(sd, c)
        ev_red = {}
        n_t = {0: 8, 1: 16}
        tiles = [(0, w1s[i * 128:(i + 1) * 128, :], i) for i in range(8)] + \
                [(1, w2s[i * 128:(i + 1) * 128, j * 4096:(j + 1) * 4096], 8 + 2 * i + j)
                 for i in range(2) for j in range(2)]
        ev_ld = {}
        ev_rd = {}
        for (wi, srcap, slot) in tiles:
            par = slot % 2
            width = srcap.shape[1]
            ev_ld[slot] = sd.dma("g", lambda g, srcap=srcap, par=par, width=width: g.dma_start(
                out=buf[:, par * 4096: par * 4096 + width], in_=srcap),
                waits=[ev_rd.get(slot - 2)])
            def _red(v, slot=slot, par=par, width=width):
                v.reduce_sum(acc[:, slot:slot + 1], buf[:, par * 4096: par * 4096 + width],
                             axis=AX, apply_absolute_value=True)
            ev_rd[slot] = sd.step("v", _red, waits=[ev_ld[slot], ev_c], signal=True)
        def _fin(v):
            v.reduce_sum(accs[:, 0:1], acc[:, 0:8], axis=AX)
            v.drain()
            v.reduce_sum(accs[:, 1:2], acc[:, 8:12], axis=AX)
        ev_fin = sd.step("v", _fin, waits=[ev_rd[11]], signal=True)
        ev_mm = sd.step("t", lambda t: t.matmul(ps[:, :], accs[:, :], c["ones_col_f"][:, 0:1],
                                                start=True, stop=True),
                        waits=[ev_fin], signal=True)
        ev_cp = sd.step("s", lambda s: s.copy(out2[:, :], ps[:, :]), waits=[ev_mm], signal=True)
        ev_wr = sd.dma("y", lambda y: y.dma_start(out=po.rearrange("a b -> b a"), in_=out2[:, :]), waits=[ev_cp])
        sd.step("y", lambda y: y.nop(), waits=[ev_wr])
        sd.run(block)
    return nc


def _emit_tern_transpose(nc, sd, c, E, ps, srcap, rows_, cols, thr_col, nthr_col, outap, bufs, tag, after=None):
    nrt = (rows_ + 127) // 128
    ev_ld = [None] * nrt
    ev_tern = [None] * nrt
    fbuf, tbuf, abuf, bbuf, stg = bufs
    for i in range(nrt):
        rr = min(128, rows_ - i * 128)
        ev_ld[i] = sd.dma("g", lambda g, i=i, rr=rr: g.dma_start(
            out=fbuf[0:rr, i * cols:i * cols + cols], in_=srcap[i * 128:i * 128 + rr, :]),
            waits=[after])
    for i in range(nrt):
        rr = min(128, rows_ - i * 128)
        def _tern(v, i=i, rr=rr):
            v.tensor_scalar(abuf[0:rr, 0:cols], fbuf[0:rr, i * cols:i * cols + cols],
                            thr_col, None, op0=ALU.is_gt)
            v.tensor_scalar(bbuf[0:rr, 0:cols], fbuf[0:rr, i * cols:i * cols + cols],
                            nthr_col, None, op0=ALU.is_lt)
            v.drain()
            v.tensor_tensor(tbuf[0:rr, i * cols:i * cols + cols], abuf[0:rr, 0:cols],
                            bbuf[0:rr, 0:cols], op=ALU.subtract)
        ev_tern[i] = sd.step("v", _tern, waits=[ev_ld[i]], signal=True)
    ev_wr = None
    ev_evac_prev = None
    for j in range(cols // 128):
        for i in range(nrt):
            rr = min(128, rows_ - i * 128)
            evt = sd.step("t", lambda t, i=i, j=j, rr=rr: t.transpose(
                ps.bitcast(BF16)[:, (i % 2) * 1024:(i % 2) * 1024 + rr],
                tbuf[0:rr, i * cols + j * 128: i * cols + j * 128 + 128],
                c["ident"][:, :]),
                waits=[ev_tern[i] if j == 0 else None,
                       ev_evac_prev], signal=True)
            ev_evac_prev = sd.step("s", lambda s, i=i, rr=rr: s.copy(
                stg[:, i * 128:i * 128 + rr], ps.bitcast(BF16)[:, (i % 2) * 1024:(i % 2) * 1024 + rr]),
                waits=[evt, ev_wr if i == 0 else None], signal=True)
        ev_wr = sd.dma("y", lambda y, j=j: y.dma_start(
            out=outap[j * 128:(j + 1) * 128, :], in_=stg[:, 0:rows_]),
            waits=[ev_evac_prev])
    return ev_wr


def build_p4b():
    nc = bass.Bass("TRN2", target_bir_lowering=False)
    w1s = nc.dram_tensor("w1s", [1024, D], F32, kind="ExternalInput").ap()
    w2s = nc.dram_tensor("w2s", [256, 8192], F32, kind="ExternalInput").ap()
    totals = nc.dram_tensor("totals", [8, 2], F32, kind="ExternalInput").ap()
    w1tb = nc.dram_tensor("w1tb", [D, 1024], BF16, kind="ExternalOutput").ap()
    w2tb = nc.dram_tensor("w2tb", [8192, 256], BF16, kind="ExternalOutput").ap()
    with ExitStack() as ctx:
        E = ctx.enter_context
        c = consts_setup(nc, E)
        fbuf = E(nc.sbuf_tensor("fbuf", [128, 8 * D], F32))
        tbuf = E(nc.sbuf_tensor("tbuf", [128, 8 * D], BF16))
        abuf = E(nc.sbuf_tensor("abuf", [128, 8192], BF16))
        bbuf = E(nc.sbuf_tensor("bbuf", [128, 8192], BF16))
        stg = E(nc.sbuf_tensor("stg", [128, 1024], BF16))
        tots = E(nc.sbuf_tensor("tots", [8, 2], F32))
        thr2 = E(nc.sbuf_tensor("thr2", [1, 2], F32))
        thrc = E(nc.sbuf_tensor("thrc", [128, 4], F32))
        ps = E(nc.psum_tensor("ps", [128, 2048], F32))
        block = E(nc.Block())
        sd = Sched(nc, E, tag="p4b")
        ev_c = emit_consts(sd, c)
        ev_t = sd.dma("s", lambda g: g.dma_start(out=tots[:, :], in_=totals[:, :]))
        ev_s1 = sd.step("t", lambda t: [
            t.matmul(ps[0:1, 0:1], tots[:, 0:1], c["ones_col_f"][0:8, 0:1], start=True, stop=True),
            t.matmul(ps[0:1, 1:2], tots[:, 1:2], c["ones_col_f"][0:8, 0:1], start=True, stop=True)][-1],
            waits=[ev_c, ev_t], signal=True)
        def _thr(s):
            return s.copy(thr2[:, :], ps[0:1, 0:2])
        ev_s2 = sd.step("s", _thr, waits=[ev_s1], signal=True)
        def _thr2(v):
            v.tensor_single_scalar(thr2[:, :], thr2[:, :], 1.0 / (8192.0 * 2048.0), op=ALU.mult)
            v.drain()
            v.tensor_single_scalar(thr2[:, :], thr2[:, :], 1e-8, op=ALU.max)
            v.drain()
            v.tensor_single_scalar(thr2[:, :], thr2[:, :], 0.5, op=ALU.mult)
        ev_s3 = sd.step("v", _thr2, waits=[ev_s2], signal=True)
        ev_s4 = sd.step("t", lambda t: t.matmul(ps[:, 4:6], c["ones_row_f"][:, :], thr2[0:1, :],
                                                start=True, stop=True), waits=[ev_s3], signal=True)
        def _thr3(s):
            return s.copy(thrc[:, 0:2], ps[:, 4:6])
        ev_s5 = sd.step("s", _thr3, waits=[ev_s4], signal=True)
        ev_thr = sd.step("v", lambda v: v.tensor_single_scalar(thrc[:, 2:4], thrc[:, 0:2], -1.0,
                                                               op=ALU.mult), waits=[ev_s5], signal=True)
        ev1 = _emit_tern_transpose(nc, sd, c, E, ps, w1s, 1024, D,
                                   thrc[:, 0:1], thrc[:, 2:3], w1tb,
                                   (fbuf, tbuf, abuf, bbuf, stg), "w1", after=ev_thr)
        ev2 = _emit_tern_transpose(nc, sd, c, E, ps, w2s, 256, 8192,
                                   thrc[:, 1:2], thrc[:, 3:4], w2tb,
                                   (fbuf, tbuf, abuf, bbuf, stg), "w2", after=ev1)
        sd.step("y", lambda y: y.nop(), waits=[ev1, ev2])
        sd.run(block)
    return nc



def build_p2(dump=False):
    nc = bass.Bass("TRN2", target_bir_lowering=False)
    xT = nc.dram_tensor("xT", [D, T], F32, kind="ExternalInput").ap()
    stT = nc.dram_tensor("stT", [2, T], F32, kind="ExternalInput").ap()
    stc = nc.dram_tensor("stc", [128, 128], F32, kind="ExternalInput").ap()
    wq = nc.dram_tensor("wq_s", [D, 256], F32, kind="ExternalInput").ap()
    wk = nc.dram_tensor("wk_s", [D, 256], F32, kind="ExternalInput").ap()
    wv = nc.dram_tensor("wv_s", [D, 256], F32, kind="ExternalInput").ap()
    bq = nc.dram_tensor("bq_s", [1, 256], F32, kind="ExternalInput").ap()
    bk = nc.dram_tensor("bk_s", [1, 256], F32, kind="ExternalInput").ap()
    bv = nc.dram_tensor("bv_s", [1, 256], F32, kind="ExternalInput").ap()
    ctx_o = nc.dram_tensor("ctx", [256, T], BF16, kind="ExternalOutput").ap()
    kind_i = "ExternalOutput" if dump else "Internal"
    denr = nc.dram_tensor("denr", [64, 512], F32, kind=kind_i if dump else "Internal").ap()
    qd = nc.dram_tensor("qd", [256, T], BF16, kind=kind_i)
    kd = nc.dram_tensor("kd", [256, T], BF16, kind=kind_i)
    vd = nc.dram_tensor("vd", [2, 128, 128 * 128], BF16, kind=kind_i)

    NTI = 32
    HL = 2
    with ExitStack() as ctx:
        E = ctx.enter_context
        c = consts_setup(nc, E)
        wqkv = E(nc.sbuf_tensor("wqkv", [128, NCH * 768], BF16))
        bqr = E(nc.sbuf_tensor("bqr", [1, 256], F32))
        bkr = E(nc.sbuf_tensor("bkr", [1, 256], F32))
        bvr = E(nc.sbuf_tensor("bvr", [1, 256], F32))
        rcall = E(nc.sbuf_tensor("rcall", [128, 128], F32))
        csr = E(nc.sbuf_tensor("csr", [1, 768], F32))
        negc = E(nc.sbuf_tensor("negc", [128, 2], F32))
        bqkc = E(nc.sbuf_tensor("bqkc", [128, 4], F32))
        bvbc = E(nc.sbuf_tensor("bvbc", [128, 256], BF16))
        chr_ = E(nc.sbuf_tensor("chr_", [1, 256], F32))
        ch2 = E(nc.sbuf_tensor("ch2", [1, 2], F32))
        xbt = E(nc.sbuf_tensor("xbt", [128, 2 * NCH * 512], BF16))
        rrow = E(nc.sbuf_tensor("rrow", [1, 2 * 512], F32))
        nmrow = E(nc.sbuf_tensor("nmrow", [1, 2 * 512], F32))
        rbc = E(nc.sbuf_tensor("rbc", [128, 2 * 512], F32))
        tqk = E(nc.sbuf_tensor("tqk", [128, 2 * 512], F32))
        qout = E(nc.sbuf_tensor("qout", [128, 2 * 4 * 512], BF16))
        vout = E(nc.sbuf_tensor("vout", [128, 2 * 4 * 256], BF16))
        qpair = E(nc.sbuf_tensor("qpair", [128, 2 * S], BF16))
        kpair = E(nc.sbuf_tensor("kpair", [128, 2 * S], BF16))
        vpair = E(nc.sbuf_tensor("vpair", [128, 2 * S], BF16))
        pt = E(nc.sbuf_tensor("pt", [128, 4 * 512], BF16))
        lnd = E(nc.sbuf_tensor("lnd", [1, 512], F32))
        recr = E(nc.sbuf_tensor("recr", [1, 2 * 512], F32))
        recs = E(nc.sbuf_tensor("recs", [128, 2 * 512], F32))
        ctxsb = E(nc.sbuf_tensor("ctxsb", [128, 2 * 512], BF16))
        ps = E(nc.psum_tensor("ps", [128, 4096], F32))
        block = E(nc.Block())
        sd = Sched(nc, E, tag="p2")


        pairs = [(0, 0), (0, 1), (1, 0), (1, 1)]
        qgs = []
        tiles = []
        tcnt = 0
        dcnt = 0
        for i in range(64):
            pi, qg = divmod(i, 16)
            b, hl = pairs[pi]
            km = 4 * (qg + 1)
            qgs.append((i, pi, b, hl, qg, km, tcnt))
            for kt in range(km):
                is_diag = kt >= 4 * qg
                tiles.append((tcnt, i, pi, b, hl, qg, kt, is_diag, dcnt))
                if is_diag:
                    dcnt += 1
                tcnt += 1
        km_of = {i: km for (i, pi, b, hl, qg, km, t0) in qgs}
        last_tile = {i: t0 + km - 1 for (i, pi, b, hl, qg, km, t0) in qgs}

        ev_c = emit_consts(sd, c)
        ev_w = sd.dma_group("g", [
            (lambda g, kch=kch, wsrc=wsrc, off=off: g.dma_start(
                out=wqkv[:, kch * 768 + off:kch * 768 + off + 256],
                in_=wsrc[kch * 128:(kch + 1) * 128, :]))
            for kch in range(NCH) for (wsrc, off) in ((wq, 0), (wk, 256), (wv, 512))])
        ev_bq = sd.dma("s", lambda g: g.dma_start(out=bqr[:, :], in_=bq[:, :]))
        ev_bk = sd.dma("s", lambda g: g.dma_start(out=bkr[:, :], in_=bk[:, :]))
        ev_bv = sd.dma("s", lambda g: g.dma_start(out=bvr[:, :], in_=bv[:, :]))
        ev_rc = sd.dma("s", lambda g: g.dma_start(out=rcall[:, :], in_=stc[:, :]))

        def _cs(t):
            for j in range(NCH):
                t.matmul(ps[0:1, 0:512], c["ones_col"][:, :], wqkv[:, j * 768:j * 768 + 512],
                         start=(j == 0), stop=(j == NCH - 1))
            for j in range(NCH):
                r = t.matmul(ps[0:1, 512:768], c["ones_col"][:, :], wqkv[:, j * 768 + 512:j * 768 + 768],
                             start=(j == 0), stop=(j == NCH - 1))
            return r
        ev_csmm = sd.step("t", _cs, waits=[ev_c, ev_w], signal=True)
        ev_csr = sd.step("s", lambda s: [s.copy(csr[0:1, 0:512], ps[0:1, 0:512]),
                                        s.copy(csr[0:1, 512:768], ps[0:1, 512:768])][-1],
                         waits=[ev_csmm], signal=True)

        def _ch(v):
            v.tensor_tensor(chr_[0:1, :], bqr[0:1, :], bkr[0:1, :], op=ALU.mult)
            v.drain()
            v.reduce_sum(ch2[0:1, 0:1], chr_[0:1, 0:128], axis=AX)
            v.drain()
            v.reduce_sum(ch2[0:1, 1:2], chr_[0:1, 128:256], axis=AX)
            v.drain()
            v.tensor_single_scalar(ch2[0:1, :], ch2[0:1, :], -ISQ_HD, op=ALU.mult)
        ev_ch2 = sd.step("v", _ch, waits=[ev_bq, ev_bk, ev_c], signal=True)

        def _bcmm(t):
            t.matmul(ps[:, 3584:3585], bqr[0:1, 0:128], c["one_f"][:, :], start=True, stop=True)
            t.matmul(ps[:, 3585:3586], bqr[0:1, 128:256], c["one_f"][:, :], start=True, stop=True)
            t.matmul(ps[:, 3586:3587], bkr[0:1, 0:128], c["one_f"][:, :], start=True, stop=True)
            t.matmul(ps[:, 3587:3588], bkr[0:1, 128:256], c["one_f"][:, :], start=True, stop=True)
            t.matmul(ps[:, 3590:3592], c["ones_row_f"][:, :], ch2[0:1, :], start=True, stop=True)
            return t.matmul(ps[:, 3072:3328], c["ones_row_f"][:, :], bvr[0:1, :], start=True, stop=True)
        ev_bcmm = sd.step("t", _bcmm, waits=[ev_ch2, ev_bv], signal=True)
        ev_setev = sd.step("s", lambda s: [s.copy(bqkc[:, :], ps[:, 3584:3588]),
                                          s.copy(negc[:, :], ps[:, 3590:3592]),
                                          s.copy(bvbc[:, :], ps[:, 3072:3328])][-1],
                           waits=[ev_bcmm], signal=True)

        ev_x = [None] * NTI
        ev_sr = [None] * NTI
        ev_rbcmm = [None] * NTI
        ev_rbc = [None] * NTI
        ev_qkmm = [None] * NTI
        ev_qk = [None] * NTI
        ev_vmm = [None] * NTI
        ev_vv = [None] * NTI
        ev_qkw = [None] * NTI
        ev_vw = [None] * NTI

        for ti in range(NTI):
            par = ti % 2
            ev_x[ti] = sd.dma_group("g", [
                (lambda g, ti=ti, par=par, kch=kch: g.dma_start(
                    out=xbt[:, (par * NCH + kch) * 512:(par * NCH + kch) * 512 + 512],
                    in_=xT[kch * 128:(kch + 1) * 128, ti * 512:(ti + 1) * 512]))
                for kch in range(NCH)],
                waits=[ev_vmm[ti - 2] if ti >= 2 else None])
            ev_sr[ti] = sd.dma_group("s", [
                (lambda g, ti=ti, par=par, r_i=r_i, dst=dst: g.dma_start(
                    out=dst[0:1, par * 512:(par + 1) * 512],
                    in_=stT[r_i:r_i + 1, ti * 512:(ti + 1) * 512]))
                for r_i, dst in ((0, rrow), (1, nmrow))],
                waits=[ev_vmm[ti - 2] if ti >= 2 else None])
            rb_off = 3072 + par * 512
            ev_rbcmm[ti] = sd.step("t", lambda t, ti=ti, par=par, rb_off=rb_off: t.matmul(
                ps[:, rb_off:rb_off + 512], c["ones_row_f"][:, :],
                rrow[0:1, par * 512:(par + 1) * 512], start=True, stop=True),
                waits=[ev_sr[ti], ev_rbc[ti - 2] if ti >= 2 else ev_setev], signal=True)
            ev_rbc[ti] = sd.step("s", lambda s, ti=ti, par=par, rb_off=rb_off: s.copy(
                rbc[:, par * 512:(par + 1) * 512], ps[:, rb_off:rb_off + 512]),
                waits=[ev_rbcmm[ti], ev_qk[ti - 2] if ti >= 2 else None], signal=True)

            def _qkmm(t, ti=ti, par=par):
                xb = lambda kch: xbt[:, (par * NCH + kch) * 512:(par * NCH + kch) * 512 + 512]
                for w_i, base in ((0, 0), (1, 256)):
                    for oc in range(2):
                        bank_off = (w_i * 2 + oc) * 512
                        for j in range(NCH):
                            t.matmul(ps[:, bank_off:bank_off + 512],
                                     wqkv[:, j * 768 + base + oc * 128: j * 768 + base + oc * 128 + 128],
                                     xb(j), start=(j == 0), stop=False)
                        r = t.matmul(ps[:, bank_off:bank_off + 512],
                                     csr[0:1, base + oc * 128:base + oc * 128 + 128],
                                     nmrow[0:1, par * 512:(par + 1) * 512], start=False, stop=True)
                return r
            ev_qkmm[ti] = sd.step("t", _qkmm,
                                  waits=[ev_x[ti], ev_sr[ti], ev_csr,
                                         ev_qk[ti - 1] if ti >= 1 else None], signal=True)

            def _qkv(v, ti=ti, par=par):
                def _t(sidx):
                    v.tensor_tensor(tqk[:, (sidx % 2) * 512:(sidx % 2) * 512 + 512],
                                    ps[:, sidx * 512:sidx * 512 + 512],
                                    rbc[:, par * 512:(par + 1) * 512], op=ALU.mult)
                def _q(sidx):
                    v.tensor_scalar(qout[:, (par * 4 + sidx) * 512:(par * 4 + sidx) * 512 + 512],
                                    tqk[:, (sidx % 2) * 512:(sidx % 2) * 512 + 512],
                                    bqkc[:, sidx:sidx + 1], None, op0=ALU.add)
                _t(0); _t(1); _q(0); _t(2); _q(1); _t(3); _q(2); _q(3)
            ev_qk[ti] = sd.step("v", _qkv,
                                waits=[ev_qkmm[ti], ev_rbc[ti], ev_setev,
                                       ev_qkw[ti - 2] if ti >= 2 else None], signal=True)

            def _vmm(t, ti=ti, par=par):
                xb = lambda kch: xbt[:, (par * NCH + kch) * 512:(par * NCH + kch) * 512 + 512]
                for sb in range(4):
                    v_off = 2048 + sb * 256
                    for j in range(NCH):
                        t.matmul(ps[:, v_off:v_off + 256],
                                 xb(j)[:, sb * 128:sb * 128 + 128],
                                 wqkv[:, j * 768 + 512:j * 768 + 768], start=(j == 0), stop=False)
                    r = t.matmul(ps[:, v_off:v_off + 256],
                                 nmrow[0:1, par * 512 + sb * 128:par * 512 + sb * 128 + 128],
                                 csr[0:1, 512:768], start=False, stop=True)
                return r
            ev_vmm[ti] = sd.step("t", _vmm,
                                 waits=[ev_x[ti], ev_sr[ti], ev_csr,
                                        ev_vv[ti - 1] if ti >= 1 else None], signal=True)

            def _vev(v, ti=ti, par=par):
                for sb in range(4):
                    v_off = 2048 + sb * 256
                    v.tensor_scalar(vout[:, (par * 4 + sb) * 256:(par * 4 + sb) * 256 + 256],
                                    ps[:, v_off:v_off + 256], rcall[:, ti * 4 + sb:ti * 4 + sb + 1],
                                    None, op0=ALU.mult)
                v.drain()
                for sb in range(4):
                    v.tensor_tensor(vout[:, (par * 4 + sb) * 256:(par * 4 + sb) * 256 + 256],
                                    vout[:, (par * 4 + sb) * 256:(par * 4 + sb) * 256 + 256],
                                    bvbc[:, :], op=ALU.add)
            ev_vv[ti] = sd.step("v", _vev,
                                waits=[ev_vmm[ti], ev_setev, ev_rc,
                                       ev_vw[ti - 2] if ti >= 2 else None], signal=True)
            ev_qkw[ti] = sd.dma_group("y", [
                (lambda y, ti=ti, par=par, sidx=sidx, dst=dst: y.dma_start(
                    out=dst[(sidx % 2) * 128:(sidx % 2) * 128 + 128, ti * 512:(ti + 1) * 512],
                    in_=qout[:, (par * 4 + sidx) * 512:(par * 4 + sidx) * 512 + 512]))
                for sidx, dst in enumerate((qd, qd, kd, kd))],
                waits=[ev_qk[ti]])
            ev_vw[ti] = sd.dma_group("y", [
                (lambda y, ti=ti, par=par, hl=hl: y.dma_start(
                    out=vd.ap()[hl][:, ti * 4 * 128:(ti + 1) * 4 * 128]
                    .rearrange("p (sb dv) -> p sb dv", dv=128),
                    in_=vout.ap().rearrange("p (pp sb h dv) -> p pp sb h dv", pp=2, sb=4, h=2)[:, par, :, hl, :]))
                for hl in range(HL)],
                waits=[ev_vv[ti]])

        ev_pl = [None] * 4
        ev_st = [None] * tcnt
        ev_ex = [None] * tcnt
        ev_msk = [None] * tcnt
        ev_ct = [None] * tcnt
        ev_rcp = [None] * 64
        ev_rbcB = [None] * 64
        ev_recs = [None] * 64
        ev_cm = [None] * 64
        ev_cw = [None] * 64

        def emit_pl(pi):
            b, hl = pairs[pi]
            par = pi % 2
            w = [ev_qkw[15 if b == 0 else 31], ev_vw[15 if b == 0 else 31]]
            if pi >= 2:
                gate = ev_ct[last_tile[(pi - 1) * 16 - 1]]
                assert gate is not None, "pl gate missing"
                w.append(gate)
            ev_pl[pi] = sd.dma_group("y", [
                lambda y, par=par, b=b, hl=hl: y.dma_start(
                    out=qpair[:, par * S:(par + 1) * S],
                    in_=qd[hl * 128:(hl + 1) * 128, b * S:(b + 1) * S]),
                lambda y, par=par, b=b, hl=hl: y.dma_start(
                    out=kpair[:, par * S:(par + 1) * S],
                    in_=kd[hl * 128:(hl + 1) * 128, b * S:(b + 1) * S]),
                lambda y, par=par, b=b, hl=hl: y.dma_start(
                    out=vpair[:, par * S:(par + 1) * S],
                    in_=vd.ap()[hl][:, b * S:(b + 1) * S])], waits=w)

        def emit_ctx(tile):
            (j, i, pi, b, hl, qg, kt, is_diag, di) = tile
            par = pi % 2
            par2 = i % 2
            def _ct(t, j=j, i=i, par=par, par2=par2, kt=kt):
                src = pt[:, (j % 4) * 512:(j % 4) * 512 + 512]
                co = 1024 + par2 * 512
                t.matmul(ps[:, co:co + 512], vpair[:, par * S + kt * 128: par * S + kt * 128 + 128],
                         src, start=(kt == 0), stop=(kt == km_of[i] - 1))
                do = 2048 + par2 * 512
                return t.matmul(ps[0:1, do:do + 512], c["ones_col"][:, :], src,
                                start=(kt == 0), stop=(kt == km_of[i] - 1))
            ev_ct[j] = sd.step("t", _ct, waits=[ev_msk[j] if is_diag else ev_ex[j]], signal=True)

        def emit_rbcB(ii):
            def _rb(t, ii=ii):
                return t.matmul(ps[:, 3072 + (ii % 2) * 512: 3072 + (ii % 2) * 512 + 512],
                         c["ones_row_f"][:, :], recr[0:1, (ii % 2) * 512:(ii % 2) * 512 + 512],
                         start=True, stop=True)
            ev_rbcB[ii] = sd.step("t", _rb,
                                  waits=[ev_rcp[ii], ev_cm[ii - 2] if ii >= 2 else None], signal=True)

        def emit_recs(ii):
            def _recs(s, ii=ii):
                return s.copy(recs[:, (ii % 2) * 512:(ii % 2) * 512 + 512],
                              ps[:, 3072 + (ii % 2) * 512:3072 + (ii % 2) * 512 + 512])
            ev_recs[ii] = sd.step("s", _recs, waits=[ev_rbcB[ii]], signal=True)
            if dump:
                sd.dma("s", lambda y, ii=ii: y.dma_start(
                    out=denr[ii:ii + 1, :], in_=recs[0:1, (ii % 2) * 512:(ii % 2) * 512 + 512]),
                    waits=[ev_recs[ii]])

        def emit_cm(ii):
            par2 = ii % 2
            def _cm(v, ii=ii, par2=par2):
                co = 1024 + par2 * 512
                v.tensor_tensor(ctxsb[:, (ii % 2) * 512:(ii % 2) * 512 + 512],
                                ps[:, co:co + 512], recs[:, par2 * 512:par2 * 512 + 512], op=ALU.mult)
            ev_cm[ii] = sd.step("v", _cm,
                                waits=[ev_recs[ii], ev_ct[last_tile[ii]],
                                       ev_cw[ii - 2] if ii >= 2 else None], signal=True)

        def emit_rcp(ii):
            par2 = ii % 2
            def _rcp(s, ii=ii, par2=par2):
                do = 2048 + par2 * 512
                s.activation(lnd[0:1, :], ps[0:1, do:do + 512], AF.Ln)
                s.drain()
                return s.activation(recr[0:1, par2 * 512:par2 * 512 + 512], lnd[0:1, :],
                                    AF.Exp, scale=c["neg_one"][0:1, 0:1])
            ev_rcp[ii] = sd.step("s", _rcp, waits=[ev_ct[last_tile[ii]]], signal=True)

        def emit_cw(ii):
            (i, pi, b, hl, qg, km, t0) = qgs[ii]
            ev_cw[ii] = sd.dma("y", lambda y, b=b, hl=hl, qg=qg, ii=ii: y.dma_start(
                out=ctx_o[hl * 128:(hl + 1) * 128, b * S + qg * 512: b * S + qg * 512 + 512],
                in_=ctxsb[:, (ii % 2) * 512:(ii % 2) * 512 + 512]),
                waits=[ev_cm[ii]])

        emit_pl(0)
        emit_pl(1)
        for tile in tiles:
            (j, i, pi, b, hl, qg, kt, is_diag, di) = tile
            par = pi % 2
            par2 = i % 2

            w = [ev_pl[pi] if kt == 0 else None]
            if kt == 0 and i >= 2:
                w.append(ev_cm[i - 2])
                w.append(ev_rcp[i - 2])
            def _st(t, j=j, par=par, qg=qg, kt=kt):
                return t.matmul(ps[:, (j % 2) * 512:(j % 2) * 512 + 512],
                         kpair[:, par * S + kt * 128: par * S + kt * 128 + 128],
                         qpair[:, par * S + qg * 512: par * S + qg * 512 + 512],
                         start=True, stop=True)
            ev_st[j] = sd.step("t", _st, waits=w, signal=True)
            def _ex(s, j=j, hl=hl):
                return s.activation(pt[:, (j % 4) * 512:(j % 4) * 512 + 512],
                             ps[:, (j % 2) * 512:(j % 2) * 512 + 512], AF.Exp,
                             bias=negc[:, hl:hl + 1], scale=c["isq_hd"][:, :1])
            ev_ex[j] = sd.step("s", _ex,
                               waits=[ev_st[j], ev_ct[j - 4] if j >= 4 else None], signal=True)
            if is_diag:
                delta = kt * 128 - qg * 512
                def _msk(v, j=j, delta=delta):
                    v.tensor_tensor(pt[:, (j % 4) * 512:(j % 4) * 512 + 512],
                                    pt[:, (j % 4) * 512:(j % 4) * 512 + 512],
                                    c["cmask"][:, 384 - delta:384 - delta + 512], op=ALU.mult)
                ev_msk[j] = sd.step("v", _msk, waits=[ev_ex[j]], signal=True)
            if kt == 1 and i >= 1:
                emit_rbcB(i - 1)
                emit_recs(i - 1)
                emit_cm(i - 1)
                emit_cw(i - 1)
                if i in (16, 32):
                    emit_pl(2 if i == 16 else 3)
            if j >= 1:
                emit_ctx(tiles[j - 1])
                pj = tiles[j - 1]
                if pj[6] == km_of[pj[1]] - 1:
                    emit_rcp(pj[1])
        emit_ctx(tiles[tcnt - 1])
        emit_rcp(63)
        emit_rbcB(63)
        emit_recs(63)
        emit_cm(63)
        emit_cw(63)
        sd.step("y", lambda y: y.nop(), waits=[ev_cw[63]])
        sd.run(block)
    return nc



def build_p5a(ntg=None):
    nc = bass.Bass("TRN2", target_bir_lowering=False)
    xng = nc.dram_tensor("xng", [D, TC], BF16, kind="ExternalInput").ap()
    w1tg = nc.dram_tensor("w1tg", [8 * D, 1024], BF16, kind="ExternalInput").ap()
    w2tg = nc.dram_tensor("w2tg", [8 * 8192, 256], BF16, kind="ExternalInput").ap()
    b1c = nc.dram_tensor("b1c", [128, 64], F32, kind="ExternalInput").ap()
    b2c = nc.dram_tensor("b2c", [128, 16], F32, kind="ExternalInput").ap()
    fout = nc.dram_tensor("fout", [D, TC], BF16, kind="ExternalOutput").ap()

    NTG = ntg if ntg else TC // 512
    NFC = 64
    with ExitStack() as ctx:
        E = ctx.enter_context
        c = consts_setup(nc, E)
        xnb = E(nc.sbuf_tensor("xnb", [128, NCH * 512], BF16))
        w1blk = E(nc.sbuf_tensor("w1blk", [128, 2 * NCH * 1024], BF16))
        w2blk = E(nc.sbuf_tensor("w2blk", [128, 2 * 2048], BF16))
        hbuf = E(nc.sbuf_tensor("hbuf", [128, NFC * 512], BF16))
        b1s = E(nc.sbuf_tensor("b1s", [128, 64], F32))
        b2s = E(nc.sbuf_tensor("b2s", [128, 16], F32))
        fst = E(nc.sbuf_tensor("fst", [128, 2 * 512], BF16))
        ps = E(nc.psum_tensor("ps", [128, 2048], F32))
        block = E(nc.Block())
        sd = Sched(nc, E, tag="p5a")

        ev_c = emit_consts(sd, c)
        ev_b1 = sd.dma("s", lambda g: g.dma_start(out=b1s[:, :], in_=b1c[:, :]))
        ev_b2 = sd.dma("s", lambda g: g.dma_start(out=b2s[:, :], in_=b2c[:, :]))

        ev_wd_holder = [None]
        ev_xn = [None] * NTG
        ev_w1 = [[None] * 8 for _ in range(NTG)]
        ev_hmm = [[None] * NFC for _ in range(NTG)]
        ev_hev = [[None] * NFC for _ in range(NTG)]
        ev_w2 = [[[None] * 8 for _ in range(8)] for _ in range(NTG)]
        ev_dmm = [[[None] * 8 for _ in range(8)] for _ in range(NTG)]
        ev_dev = [[None] * NCH for _ in range(NTG)]
        ev_fw = [[None] * NCH for _ in range(NTG)]

        def prev_lin(tg, a, b_, n2, arr):
            l = a * n2 + b_ - 2
            if l >= 0:
                return arr[tg][l // n2][l % n2]
            if tg >= 1:
                l += 8 * n2
                return arr[tg - 1][l // n2][l % n2]
            return None

        for tg in range(NTG):
            ev_xn[tg] = sd.dma_group("g", [
                (lambda g, tg=tg, ic=ic: g.dma_start(
                    out=xnb[:, ic * 512:(ic + 1) * 512],
                    in_=xng[ic * 128:(ic + 1) * 128, tg * 512:(tg + 1) * 512]))
                for ic in range(NCH)],
                waits=[ev_hmm[tg - 1][NFC - 1] if tg >= 1 else None])
            def emit_w1_load(r, tg=tg):
                par = r % 2
                if r >= 2:
                    prev = ev_hmm[tg][8 * (r - 2) + 7]
                    assert prev is not None
                elif tg >= 1:
                    prev = ev_hmm[tg - 1][8 * (6 + r) + 7]
                else:
                    prev = None
                ev_w1[tg][r] = sd.dma_group("g", [
                    (lambda g, tg=tg, r=r, par=par, ic=ic: g.dma_start(
                        out=w1blk[:, (par * NCH + ic) * 1024:(par * NCH + ic) * 1024 + 1024],
                        in_=w1tg[2048 * r + ic * 128:2048 * r + (ic + 1) * 128, :]))
                    for ic in range(NCH)],
                    waits=[prev])
            emit_w1_load(0)
            emit_w1_load(1)
            for fc in range(NFC):
                r = fc // 8
                par = r % 2
                if fc % 8 == 0 and r + 1 < 8 and r >= 1:
                    emit_w1_load(r + 1)
                def _hmm(t, tg=tg, fc=fc, par=par):
                    pb = (fc % 2) * 512
                    for ic in range(NCH):
                        r = t.matmul(ps[:, pb:pb + 512],
                                     w1blk[:, (par * NCH + ic) * 1024 + (fc % 8) * 128:
                                           (par * NCH + ic) * 1024 + (fc % 8) * 128 + 128],
                                     xnb[:, ic * 512:(ic + 1) * 512],
                                     start=(ic == 0), stop=(ic == NCH - 1))
                    return r
                ev_hmm[tg][fc] = sd.step("t", _hmm,
                                         waits=[ev_xn[tg], ev_w1[tg][r],
                                                ev_hev[tg][fc - 2] if fc >= 2 else (
                                                    ev_hev[tg - 1][NFC - 2 + fc] if tg >= 1 else None)],
                                         signal=True)
                ev_hev[tg][fc] = sd.step("s", lambda s, tg=tg, fc=fc: s.activation(
                    hbuf[:, fc * 512:(fc + 1) * 512], ps[:, (fc % 2) * 512:(fc % 2) * 512 + 512],
                    AF.Gelu, bias=b1s[:, fc:fc + 1]),
                    waits=[ev_hmm[tg][fc], ev_b1,
                           ev_dmm[tg - 1][7][fc // 8] if tg >= 1 else None], signal=True)
            for r2 in range(8):
                for fcg in range(8):
                    ev_w2[tg][r2][fcg] = sd.dma("g", lambda g, tg=tg, r2=r2, fcg=fcg: g.dma_start(
                        out=w2blk[:, (fcg % 2) * 2048:(fcg % 2) * 2048 + 2048]
                        .rearrange("p (f c) -> p f c", f=8),
                        in_=w2tg[8192 * r2 + 1024 * fcg:8192 * r2 + 1024 * (fcg + 1), :]
                        .rearrange("(f p) c -> p f c", p=128)),
                        waits=[prev_lin(tg, r2, fcg, 8, ev_dmm)])
                    def _dmm(t, tg=tg, r2=r2, fcg=fcg):
                        for fi in range(8):
                            fc = fcg * 8 + fi
                            for ocl in range(2):
                                r = t.matmul(ps[:, 1024 + ocl * 512:1024 + ocl * 512 + 512],
                                             w2blk[:, (fcg % 2) * 2048 + fi * 256 + ocl * 128:
                                                   (fcg % 2) * 2048 + fi * 256 + ocl * 128 + 128],
                                             hbuf[:, fc * 512:(fc + 1) * 512],
                                             start=(fc == 0), stop=(fc == NFC - 1))
                        return r
                    w = [ev_w2[tg][r2][fcg], ev_hev[tg][8 * fcg + 7]]
                    if fcg == 0:
                        w.append(ev_dev[tg][2 * (r2 - 1) + 1] if r2 >= 1 else (
                            ev_dev[tg - 1][NCH - 1] if tg >= 1 else None))
                    ev_dmm[tg][r2][fcg] = sd.step("t", _dmm, waits=w, signal=True)
                for ocl in range(2):
                    oc = 2 * r2 + ocl
                    def _dev(v, tg=tg, oc=oc, ocl=ocl):
                        v.tensor_scalar(fst[:, (oc % 2) * 512:(oc % 2) * 512 + 512],
                                        ps[:, 1024 + ocl * 512:1024 + ocl * 512 + 512],
                                        b2s[:, oc:oc + 1], None, op0=ALU.add)
                    ev_dev[tg][oc] = sd.step("v", _dev,
                                             waits=[ev_dmm[tg][r2][7], ev_b2,
                                                    prev_lin(tg, oc // 2, oc % 2, 2, None) if False else (
                                                        ev_fw[tg][oc - 2] if oc >= 2 else (
                                                            ev_fw[tg - 1][NCH - 1] if tg >= 1 else None))],
                                             signal=True)
                    ev_fw[tg][oc] = sd.dma("y", lambda y, tg=tg, oc=oc: y.dma_start(
                        out=fout[oc * 128:(oc + 1) * 128, tg * 512:(tg + 1) * 512],
                        in_=fst[:, (oc % 2) * 512:(oc % 2) * 512 + 512]),
                        waits=[ev_dev[tg][oc]])
        sd.step("y", lambda y: y.nop(), waits=[ev_fw[NTG - 1][NCH - 1]])
        sd.run(block)
    return nc



def _emit_cast_transpose(sd, c, ps, srcap, ibuf, tbuf, stg, rows_, cols, outap, after=None):
    nrt = (rows_ + 127) // 128
    ev_ld = [None] * nrt
    ev_cast = [None] * nrt
    for i in range(nrt):
        rr = min(128, rows_ - i * 128)
        ev_ld[i] = sd.dma("g", lambda g, i=i, rr=rr: g.dma_start(
            out=ibuf[0:rr, i * cols:i * cols + cols], in_=srcap[i * 128:i * 128 + rr, :]),
            waits=[after])
    for i in range(nrt):
        rr = min(128, rows_ - i * 128)
        ev_cast[i] = sd.step("v", lambda v, i=i, rr=rr: v.tensor_single_scalar(
            tbuf[0:rr, i * cols:i * cols + cols], ibuf[0:rr, i * cols:i * cols + cols],
            -128, op=ALU.add), waits=[ev_ld[i]], signal=True)
    ev_wr = None
    ev_evac_prev = None
    for j in range(cols // 128):
        for i in range(nrt):
            rr = min(128, rows_ - i * 128)
            evt = sd.step("t", lambda t, i=i, j=j, rr=rr: t.transpose(
                ps.bitcast(BF16)[:, (i % 2) * 1024:(i % 2) * 1024 + rr],
                tbuf[0:rr, i * cols + j * 128: i * cols + j * 128 + 128],
                c["ident"][0:rr, 0:rr]),
                waits=[ev_cast[i] if j == 0 else None, ev_evac_prev], signal=True)
            ev_evac_prev = sd.step("s", lambda s, i=i, rr=rr: s.copy(
                stg[:, i * 128:i * 128 + rr], ps.bitcast(BF16)[:, (i % 2) * 1024:(i % 2) * 1024 + rr]),
                waits=[evt, ev_wr if i == 0 else None], signal=True)
        ev_wr = sd.dma("y", lambda y, j=j: y.dma_start(
            out=outap[j * 128:(j + 1) * 128, :], in_=stg[:, 0:rows_]),
            waits=[ev_evac_prev])
    return ev_wr


def build_p6():
    nc = bass.Bass("TRN2", target_bir_lowering=False)
    gs = nc.dram_tensor("gshard", [768, D], I32, kind="ExternalInput").ap()
    us = nc.dram_tensor("ushard", [768, D], I32, kind="ExternalInput").ap()
    ds = nc.dram_tensor("dshard", [256, 6144], I32, kind="ExternalInput").ap()
    wgT = nc.dram_tensor("wgT", [D, 768], BF16, kind="ExternalOutput").ap()
    wuT = nc.dram_tensor("wuT", [D, 768], BF16, kind="ExternalOutput").ap()
    wdT = nc.dram_tensor("wdT", [6144, 256], BF16, kind="ExternalOutput").ap()
    with ExitStack() as ctx:
        E = ctx.enter_context
        c = consts_setup(nc, E)
        ibuf = E(nc.sbuf_tensor("ibuf", [128, 6 * D], I32))
        tbuf = E(nc.sbuf_tensor("tbuf", [128, 6 * D], BF16))
        stg = E(nc.sbuf_tensor("stg", [128, 1024], BF16))
        ps = E(nc.psum_tensor("ps", [128, 2048], F32))
        block = E(nc.Block())
        sd = Sched(nc, E, tag="p6")
        ev_c = emit_consts(sd, c)
        sd.step("v", lambda v: v.memset(tbuf[:, 0:8], 0.0), waits=[ev_c], signal=True)
        ev1 = _emit_cast_transpose(sd, c, ps, gs, ibuf, tbuf, stg, 768, D, wgT)
        ev2 = _emit_cast_transpose(sd, c, ps, us, ibuf, tbuf, stg, 768, D, wuT, after=ev1)
        ev3 = _emit_cast_transpose(sd, c, ps, ds, ibuf[:, 0:2 * 6144],
                                   tbuf[:, 0:2 * 6144], stg, 256, 6144, wdT, after=ev2)
        sd.step("y", lambda y: y.nop(), waits=[ev3])
        sd.run(block)
    return nc



def build_p9(ntg=None):
    HID = 6144
    NHC = 48
    nc = bass.Bass("TRN2", target_bir_lowering=False)
    xng = nc.dram_tensor("xng", [D, TC], BF16, kind="ExternalInput").ap()
    wgTg = nc.dram_tensor("wgTg", [8 * D, 768], BF16, kind="ExternalInput").ap()
    wuTg = nc.dram_tensor("wuTg", [8 * D, 768], BF16, kind="ExternalInput").ap()
    wdTg = nc.dram_tensor("wdTg", [8 * HID, 256], BF16, kind="ExternalInput").ap()
    sgc = nc.dram_tensor("sgc", [128, NHC], F32, kind="ExternalInput").ap()
    suc = nc.dram_tensor("suc", [128, NHC], F32, kind="ExternalInput").ap()
    sdc = nc.dram_tensor("sdc", [128, NCH], F32, kind="ExternalInput").ap()
    fout = nc.dram_tensor("fout", [D, TC], BF16, kind="ExternalOutput").ap()

    NTG = ntg if ntg else TC // 512
    with ExitStack() as ctx:
        E = ctx.enter_context
        c = consts_setup(nc, E)
        xnb = E(nc.sbuf_tensor("xnb", [128, NCH * 512], BF16))
        wblk = E(nc.sbuf_tensor("wblk", [128, 2 * NCH * 768], BF16))
        wdblk = E(nc.sbuf_tensor("wdblk", [128, 2 * 2048], BF16))
        gbuf = E(nc.sbuf_tensor("gbuf", [128, NHC * 512], BF16))
        tstg = E(nc.sbuf_tensor("tstg", [128, 2 * 512], BF16))
        sgs = E(nc.sbuf_tensor("sgs", [128, NHC], F32))
        sus = E(nc.sbuf_tensor("sus", [128, NHC], F32))
        sds = E(nc.sbuf_tensor("sds", [128, NCH], F32))
        fst = E(nc.sbuf_tensor("fst", [128, 2 * 512], BF16))
        ps = E(nc.psum_tensor("ps", [128, 2048], F32))
        block = E(nc.Block())
        sd = Sched(nc, E, tag="p9")

        ev_c = emit_consts(sd, c)
        ev_sg = sd.dma("s", lambda g: g.dma_start(out=sgs[:, :], in_=sgc[:, :]))
        ev_su = sd.dma("s", lambda g: g.dma_start(out=sus[:, :], in_=suc[:, :]))
        ev_sd2 = sd.dma("s", lambda g: g.dma_start(out=sds[:, :], in_=sdc[:, :]))

        ev_xn = [None] * NTG
        ev_wg = [[None] * 8 for _ in range(NTG)]
        ev_gmm = [[None] * NHC for _ in range(NTG)]
        ev_gev = [[None] * NHC for _ in range(NTG)]
        ev_umm = [[None] * NHC for _ in range(NTG)]
        ev_uev = [[None] * NHC for _ in range(NTG)]
        ev_wd = [[[None] * 6 for _ in range(8)] for _ in range(NTG)]
        ev_dmm = [[[None] * 6 for _ in range(8)] for _ in range(NTG)]
        ev_dev = [[None] * NCH for _ in range(NTG)]
        ev_fw = [[None] * NCH for _ in range(NTG)]

        for tg in range(NTG):
            ev_xn[tg] = sd.dma_group("g", [
                (lambda g, tg=tg, ic=ic: g.dma_start(
                    out=xnb[:, ic * 512:(ic + 1) * 512],
                    in_=xng[ic * 128:(ic + 1) * 128, tg * 512:(tg + 1) * 512]))
                for ic in range(NCH)],
                waits=[ev_umm[tg - 1][NHC - 1] if tg >= 1 else None])

            def emit_wload(r, wsrc, ev_mm_arr, tg=tg):
                par = r % 2
                if r >= 2:
                    prev = ev_mm_arr[tg][6 * (r - 2) + 5]
                    assert prev is not None
                else:
                    prev = None
                return sd.dma_group("g", [
                    (lambda g, r=r, par=par, ic=ic, wsrc=wsrc: g.dma_start(
                        out=wblk[:, (par * NCH + ic) * 768:(par * NCH + ic) * 768 + 768],
                        in_=wsrc[2048 * r + ic * 128:2048 * r + (ic + 1) * 128, :]))
                    for ic in range(NCH)],
                    waits=[prev])

            prev_guard = ev_umm[tg - 1][NHC - 1] if tg >= 1 else None
            for r in (0, 1):
                ev_wg[tg][r] = sd.dma_group("g", [
                    (lambda g, r=r, ic=ic: g.dma_start(
                        out=wblk[:, ((r % 2) * NCH + ic) * 768:((r % 2) * NCH + ic) * 768 + 768],
                        in_=wgTg[2048 * r + ic * 128:2048 * r + (ic + 1) * 128, :]))
                    for ic in range(NCH)],
                    waits=[prev_guard])
            for hc in range(NHC):
                r = hc // 6
                par = r % 2
                if hc % 6 == 0 and 2 <= r + 1 < 8:
                    ev_wg[tg][r + 1] = emit_wload(r + 1, wgTg, ev_gmm)
                def _gmm(t, tg=tg, hc=hc, par=par):
                    pb = (hc % 2) * 512
                    for ic in range(NCH):
                        rr = t.matmul(ps[:, pb:pb + 512],
                                      wblk[:, (par * NCH + ic) * 768 + (hc % 6) * 128:
                                            (par * NCH + ic) * 768 + (hc % 6) * 128 + 128],
                                      xnb[:, ic * 512:(ic + 1) * 512],
                                      start=(ic == 0), stop=(ic == NCH - 1))
                    return rr
                ev_gmm[tg][hc] = sd.step("t", _gmm,
                                         waits=[ev_xn[tg], ev_wg[tg][r],
                                                ev_gev[tg][hc - 2] if hc >= 2 else (
                                                    ev_uev[tg - 1][NHC - 2 + hc] if tg >= 1 else None)],
                                         signal=True)
                ev_gev[tg][hc] = sd.step("s", lambda s, tg=tg, hc=hc: s.activation(
                    gbuf[:, hc * 512:(hc + 1) * 512], ps[:, (hc % 2) * 512:(hc % 2) * 512 + 512],
                    AF.Silu, scale=sgs[:, hc:hc + 1]),
                    waits=[ev_gmm[tg][hc], ev_sg,
                           ev_dmm[tg - 1][7][hc // 8] if tg >= 1 else None], signal=True)
            for r in (0, 1):
                ev_wg[tg][r] = sd.dma_group("g", [
                    (lambda g, r=r, ic=ic: g.dma_start(
                        out=wblk[:, ((r % 2) * NCH + ic) * 768:((r % 2) * NCH + ic) * 768 + 768],
                        in_=wuTg[2048 * r + ic * 128:2048 * r + (ic + 1) * 128, :]))
                    for ic in range(NCH)],
                    waits=[ev_gmm[tg][6 * r + 41]])
            for hc in range(NHC):
                r = hc // 6
                par = r % 2
                if hc % 6 == 0 and 2 <= r + 1 < 8:
                    ev_wg[tg][r + 1] = emit_wload(r + 1, wuTg, ev_umm)
                def _umm(t, tg=tg, hc=hc, par=par):
                    pb = (hc % 2) * 512
                    for ic in range(NCH):
                        rr = t.matmul(ps[:, pb:pb + 512],
                                      wblk[:, (par * NCH + ic) * 768 + (hc % 6) * 128:
                                            (par * NCH + ic) * 768 + (hc % 6) * 128 + 128],
                                      xnb[:, ic * 512:(ic + 1) * 512],
                                      start=(ic == 0), stop=(ic == NCH - 1))
                    return rr
                ev_umm[tg][hc] = sd.step("t", _umm,
                                         waits=[ev_wg[tg][r],
                                                ev_uev[tg][hc - 2] if hc >= 2 else ev_gev[tg][NHC - 1]],
                                         signal=True)
                def _uev(v, tg=tg, hc=hc):
                    v.tensor_scalar(tstg[:, (hc % 2) * 512:(hc % 2) * 512 + 512],
                                    ps[:, (hc % 2) * 512:(hc % 2) * 512 + 512],
                                    sus[:, hc:hc + 1], None, op0=ALU.mult)
                    v.drain()
                    v.tensor_tensor(gbuf[:, hc * 512:(hc + 1) * 512],
                                    gbuf[:, hc * 512:(hc + 1) * 512],
                                    tstg[:, (hc % 2) * 512:(hc % 2) * 512 + 512], op=ALU.mult)
                ev_uev[tg][hc] = sd.step("v", _uev,
                                         waits=[ev_umm[tg][hc], ev_su, ev_gev[tg][hc]], signal=True)
            for r2 in range(8):
                for hg in range(6):
                    li = r2 * 6 + hg
                    if li >= 2:
                        prevd = ev_dmm[tg][(li - 2) // 6][(li - 2) % 6]
                    elif tg >= 1:
                        prevd = ev_dmm[tg - 1][(46 + li) // 6][(46 + li) % 6]
                    else:
                        prevd = None
                    ev_wd[tg][r2][hg] = sd.dma("g", lambda g, tg=tg, r2=r2, hg=hg: g.dma_start(
                        out=wdblk[:, (hg % 2) * 2048:(hg % 2) * 2048 + 2048]
                        .rearrange("p (f c) -> p f c", f=8),
                        in_=wdTg[HID * r2 + 1024 * hg:HID * r2 + 1024 * (hg + 1), :]
                        .rearrange("(f p) c -> p f c", p=128)),
                        waits=[prevd])
                    def _dmm(t, tg=tg, r2=r2, hg=hg):
                        for fi in range(8):
                            hc = hg * 8 + fi
                            for ocl in range(2):
                                rr = t.matmul(ps[:, 1024 + ocl * 512:1024 + ocl * 512 + 512],
                                              wdblk[:, (hg % 2) * 2048 + fi * 256 + ocl * 128:
                                                    (hg % 2) * 2048 + fi * 256 + ocl * 128 + 128],
                                              gbuf[:, hc * 512:(hc + 1) * 512],
                                              start=(hc == 0), stop=(hc == NHC - 1))
                        return rr
                    w = [ev_wd[tg][r2][hg], ev_uev[tg][min(8 * hg + 7, NHC - 1)]]
                    if hg == 0:
                        w.append(ev_dev[tg][2 * (r2 - 1) + 1] if r2 >= 1 else (
                            ev_dev[tg - 1][NCH - 1] if tg >= 1 else None))
                    ev_dmm[tg][r2][hg] = sd.step("t", _dmm, waits=w, signal=True)
                for ocl in range(2):
                    oc = 2 * r2 + ocl
                    def _dev(v, tg=tg, oc=oc, ocl=ocl):
                        v.tensor_scalar(fst[:, (oc % 2) * 512:(oc % 2) * 512 + 512],
                                        ps[:, 1024 + ocl * 512:1024 + ocl * 512 + 512],
                                        sds[:, oc:oc + 1], None, op0=ALU.mult)
                    ev_dev[tg][oc] = sd.step("v", _dev,
                                             waits=[ev_dmm[tg][r2][5], ev_sd2,
                                                    ev_fw[tg][oc - 2] if oc >= 2 else (
                                                        ev_fw[tg - 1][NCH - 1] if tg >= 1 else None)],
                                             signal=True)
                    ev_fw[tg][oc] = sd.dma("y", lambda y, tg=tg, oc=oc: y.dma_start(
                        out=fout[oc * 128:(oc + 1) * 128, tg * 512:(tg + 1) * 512],
                        in_=fst[:, (oc % 2) * 512:(oc % 2) * 512 + 512]),
                        waits=[ev_dev[tg][oc]])
        sd.step("y", lambda y: y.nop(), waits=[ev_fw[NTG - 1][NCH - 1]])
        sd.run(block)
    return nc



def build_p7(nbc=256, upto=None):
    import ml_dtypes as mld
    nc = bass.Bass("TRN2", target_bir_lowering=False)
    zng = nc.dram_tensor("zng", [D, T], BF16, kind="ExternalInput").ap()
    wqs = nc.dram_tensor("wq_s", [D, 256], F32, kind="ExternalInput").ap()
    wks = nc.dram_tensor("wk_s", [D, 256], F32, kind="ExternalInput").ap()
    wvs = nc.dram_tensor("wv_s", [D, 256], F32, kind="ExternalInput").ap()
    wbs = nc.dram_tensor("wb_s", [D, 2], F32, kind="ExternalInput").ap()
    tapc = nc.dram_tensor("tapc", [128, 12], F32, kind="ExternalInput").ap()
    onw = nc.dram_tensor("onw", [1, 128], F32, kind="ExternalInput").ap()
    ofm = nc.dram_tensor("ofm", [256, T], BF16, kind="ExternalOutput").ap()
    qkvd = nc.dram_tensor("qkvd", [T, 768], BF16)

    NTT = T // 128
    with ExitStack() as ctx:
        E = ctx.enter_context
        c = consts_setup(nc, E)
        diags = []
        for s_ in range(4):
            m = np.zeros((128, 128), np.float32)
            for t_ in range(128 - s_):
                m[t_, t_ + s_] = 1.0
            diags.append(m)
        dg_d = nc.inline_tensor(np.concatenate(diags, 1).astype(mld.bfloat16), name="c_diags")
        mcar = np.zeros((3, 3 * 128), np.float32)
        for d_ in range(3):
            for i_ in range(3):
                t_ = i_ - d_
                if 0 <= t_ < 128:
                    mcar[i_, d_ * 128 + t_] = 1.0
        mc_d = nc.inline_tensor(mcar.astype(mld.bfloat16), name="c_mcar")
        mL = np.tril(np.ones((128, 128), np.float32), -1)
        mUi = np.triu(np.ones((128, 128), np.float32), 0)
        msk_d = nc.inline_tensor(np.concatenate([mL, mUi], 1).astype(mld.bfloat16), name="c_msks")
        mbd = np.zeros((128, 128), np.float32)
        for j_ in range(4):
            mbd[j_ * 32:(j_ + 1) * 32, j_ * 32:(j_ + 1) * 32] = np.tril(np.ones((32, 32)), -1)
        moff = np.tril(np.ones((128, 128), np.float32), -1) - mbd
        mskf_d = nc.inline_tensor(np.concatenate([mbd, mbd.T, moff.T, mUi], 1).astype(np.float32),
                                  name="c_mskf")
        px_d = nc.inline_tensor(np.tile(np.array(
            [[1e-6, 1.28e-4, 128.0, 1.0 / 128, 1e-5]], np.float32), (128, 1)), name="c_px")
        dgs = E(nc.sbuf_tensor("dgs", [128, 4 * 128], BF16))
        mcs = E(nc.sbuf_tensor("mcs", [3, 3 * 128], BF16))
        msks = E(nc.sbuf_tensor("msks", [128, 2 * 128], BF16))
        tapb = E(nc.sbuf_tensor("tapb", [128, 12], F32))
        px = E(nc.sbuf_tensor("px", [128, 5], F32))
        onwr = E(nc.sbuf_tensor("onwr", [1, 128], F32))
        wbc = E(nc.sbuf_tensor("wbc", [128, 128], F32))
        Bm = E(nc.sbuf_tensor("Bm", [128, 3 * 128], BF16))
        Bc = E(nc.sbuf_tensor("Bc", [3, 3 * 128], BF16))
        btmp = E(nc.sbuf_tensor("btmp", [128, 128], BF16))
        wqkvb = E(nc.sbuf_tensor("wqkvb", [128, NCH * 770], BF16))
        znb = E(nc.sbuf_tensor("znb", [128, 2 * NCH * 1024], BF16))
        qkvw = E(nc.sbuf_tensor("qkvw", [128, 2 * 768], BF16))
        betab = E(nc.sbuf_tensor("betab", [128, 2 * NTT], F32))
        shb = E(nc.sbuf_tensor("shb", [128, 768], BF16))
        carb = E(nc.sbuf_tensor("carb", [3, 768], BF16))
        cvo = E(nc.sbuf_tensor("cvo", [128, 768], BF16))
        sqs = E(nc.sbuf_tensor("sqs", [128, 256], BF16))
        rqk = E(nc.sbuf_tensor("rqk", [128, 4], F32))
        rmsc = E(nc.sbuf_tensor("rmsc", [128, 2], F32))
        scn = E(nc.sbuf_tensor("scn", [128, 1152], BF16))
        t1f = E(nc.sbuf_tensor("t1f", [128, 128], F32))
        t1T = E(nc.sbuf_tensor("t1T", [128, 128], F32))
        BDf = E(nc.sbuf_tensor("BDf", [128, 128], F32))
        Z1f = E(nc.sbuf_tensor("Z1f", [128, 128], F32))
        AoNT = E(nc.sbuf_tensor("AoNT", [128, 128], F32))
        Lc = E(nc.sbuf_tensor("Lc", [128, 256], F32))
        Zc = E(nc.sbuf_tensor("Zc", [128, 256], F32))
        Pf = E(nc.sbuf_tensor("Pf", [128, 128], F32))
        TdT = E(nc.sbuf_tensor("TdT", [128, 128], F32))
        yf = E(nc.sbuf_tensor("yf", [128, 256], F32))
        rtf = E(nc.sbuf_tensor("rtf", [128, 256], F32))
        unf = E(nc.sbuf_tensor("unf", [128, 256], F32))
        nbc1 = E(nc.sbuf_tensor("nbc1", [128, 1], F32))
        u0f = E(nc.sbuf_tensor("u0f", [128, 128], F32))
        mskf = E(nc.sbuf_tensor("mskf", [128, 4 * 128], F32))
        Ssb = E(nc.sbuf_tensor("Ssb", [128, 8 * 128], F32))
        Sbf = E(nc.sbuf_tensor("Sbf", [128, 4 * 128], BF16))
        osb = E(nc.sbuf_tensor("osb", [128, 128], BF16))
        otmp = E(nc.sbuf_tensor("otmp", [128, 128], F32))
        ost = E(nc.sbuf_tensor("ost", [128, 2 * 128], BF16))
        ps = E(nc.psum_tensor("ps", [128, 4096], F32))
        psb = ps.bitcast(BF16)
        block = E(nc.Block())
        sd = Sched(nc, E, tag="p7")

        SC = dict(qs=0, kh=128, u0=256, wtm=384, qsf=512, khf=640, wfm=768,
                  u=896, attn=1024)

        ev_c = emit_consts(sd, c)
        ev_x1 = sd.dma("s", lambda g: g.dma_start(out=dgs[:, :], in_=dg_d.ap()))
        ev_x2 = sd.dma("s", lambda g: g.dma_start(out=mcs[:, :], in_=mc_d.ap()))
        ev_x3 = sd.dma("s", lambda g: g.dma_start(out=msks[:, :], in_=msk_d.ap()))
        ev_x4 = sd.dma("s", lambda g: g.dma_start(out=tapb[:, :], in_=tapc[:, :]))
        ev_x5 = sd.dma("s", lambda g: g.dma_start(out=onwr[:, :], in_=onw[:, :]))
        ev_x6 = sd.dma("s", lambda g: g.dma_start(out=px[:, :], in_=px_d.ap()))
        ev_x7 = sd.dma("s", lambda g: g.dma_start(out=mskf[:, :], in_=mskf_d.ap()))
        ev_wq = sd.dma_group("g", [
            (lambda g, kch=kch, wsrc=wsrc, off=off, wd=wd: g.dma_start(
                out=wqkvb[:, kch * 770 + off:kch * 770 + off + wd],
                in_=wsrc[kch * 128:(kch + 1) * 128, :]))
            for kch in range(NCH) for (wsrc, off, wd) in
            ((wqs, 0, 256), (wks, 256, 256), (wvs, 512, 256), (wbs, 768, 2))])

        def _bld_B(v):
            for ti in range(3):
                for s_ in range(4):
                    v.tensor_scalar(btmp[:, :], dgs[:, s_ * 128:(s_ + 1) * 128],
                                    tapb[:, ti * 4 + 3 - s_:ti * 4 + 4 - s_], None, op0=ALU.mult)
                    v.drain()
                    if s_ == 0:
                        v.tensor_copy(Bm[:, ti * 128:(ti + 1) * 128], btmp[:, :])
                    else:
                        v.tensor_tensor(Bm[:, ti * 128:(ti + 1) * 128],
                                        Bm[:, ti * 128:(ti + 1) * 128], btmp[:, :], op=ALU.add)
                    v.drain()
                for d_ in range(3):
                    v.tensor_scalar(btmp[0:3, 0:128], mcs[:, d_ * 128:(d_ + 1) * 128],
                                    tapb[0:3, ti * 4 + d_:ti * 4 + d_ + 1], None, op0=ALU.mult)
                    v.drain()
                    if d_ == 0:
                        v.tensor_copy(Bc[:, ti * 128:(ti + 1) * 128], btmp[0:3, 0:128])
                    else:
                        v.tensor_tensor(Bc[:, ti * 128:(ti + 1) * 128],
                                        Bc[:, ti * 128:(ti + 1) * 128], btmp[0:3, 0:128], op=ALU.add)
                    v.drain()
        ev_B = sd.step("v", _bld_B, waits=[ev_c, ev_x1, ev_x2, ev_x4, ev_x7], signal=True)
        ev_wbcm = sd.step("t", lambda t: t.matmul(ps[:, 3072:3200], c["ones_col_f"][0:1, 0:1]
                                                  .rearrange("a b -> b a") if False else
                                                  c["ones_row_f"][:, :], onwr[0:1, :],
                                                  start=True, stop=True),
                          waits=[ev_c, ev_x5], signal=True)
        ev_wbc = sd.step("s", lambda s: s.copy(wbc[:, :], ps[:, 3072:3200]),
                         waits=[ev_wbcm], signal=True)

        ev_zn = [None] * (NTT // 8)
        ev_amm = [None] * NTT
        ev_aev = [None] * NTT
        ev_awr = [None] * NTT
        for tt in range(NTT):
            g8 = tt // 8
            par8 = g8 % 2
            if tt % 8 == 0:
                ev_zn[g8] = sd.dma_group("g", [
                    (lambda g, g8=g8, par8=par8, kch=kch: g.dma_start(
                        out=znb[:, (par8 * NCH + kch) * 1024:(par8 * NCH + kch) * 1024 + 1024],
                        in_=zng[kch * 128:(kch + 1) * 128, g8 * 1024:(g8 + 1) * 1024]))
                    for kch in range(NCH)],
                    waits=[ev_amm[(g8 - 2) * 8 + 7] if g8 >= 2 else None])
            par = tt % 2
            def _amm(t, tt=tt, g8=g8, par8=par8, par=par):
                pa = par * 1024
                for kch in range(NCH):
                    zt = znb[:, (par8 * NCH + kch) * 1024 + (tt % 8) * 128:
                             (par8 * NCH + kch) * 1024 + (tt % 8) * 128 + 128]
                    t.matmul(ps[:, pa:pa + 512], zt, wqkvb[:, kch * 770:kch * 770 + 512],
                             start=(kch == 0), stop=(kch == NCH - 1))
                rr = None
                for kch in range(NCH):
                    zt = znb[:, (par8 * NCH + kch) * 1024 + (tt % 8) * 128:
                             (par8 * NCH + kch) * 1024 + (tt % 8) * 128 + 128]
                    rr = t.matmul(ps[:, pa + 512:pa + 512 + 258],
                                  zt, wqkvb[:, kch * 770 + 512:kch * 770 + 770],
                                  start=(kch == 0), stop=(kch == NCH - 1))
                return rr
            ev_amm[tt] = sd.step("t", _amm,
                                 waits=[ev_zn[g8], ev_wq,
                                        ev_aev[tt - 2] if tt >= 2 else None], signal=True)
            def _aev(s, tt=tt, par=par):
                pa = par * 1024
                s.copy(qkvw[:, (tt % 2) * 768:(tt % 2) * 768 + 512], ps[:, pa:pa + 512])
                s.drain()
                s.copy(qkvw[:, (tt % 2) * 768 + 512:(tt % 2) * 768 + 768], ps[:, pa + 512:pa + 768])
                s.drain()
                return s.activation(betab[:, tt * 2:tt * 2 + 2], ps[:, pa + 768:pa + 770], AF.Sigmoid)
            ev_aev[tt] = sd.step("s", _aev,
                                 waits=[ev_amm[tt], ev_awr[tt - 2] if tt >= 2 else None], signal=True)
            ev_awr[tt] = sd.dma("y", lambda y, tt=tt: y.dma_start(
                out=qkvd.ap()[tt * 128:(tt + 1) * 128, :],
                in_=qkvw[:, (tt % 2) * 768:(tt % 2) * 768 + 768]),
                waits=[ev_aev[tt]])

        pairs = [(0, 0), (0, 1), (1, 0), (1, 1)]
        ev_ow_hist = []
        ev_fin = None
        nemit = 0
        for pi, (b, hl) in enumerate(pairs):
            ev_s0 = sd.step("v", lambda v, pi=pi: [
                v.memset(Ssb[:, pi * 256:pi * 256 + 256], 0.0),
                v.memset(Sbf[:, pi * 128:(pi + 1) * 128], 0.0)],
                waits=[ev_wbc], signal=True)
            prev_chunk = [ev_s0]
            for ci in range(64):
                if nemit >= nbc:
                    break
                nemit += 1
                ttg = b * 64 + ci
                bcol = betab[:, ttg * 2 + hl:ttg * 2 + hl + 1]
                ev_ld = sd.dma_group("g", [
                    lambda g, ttg=ttg: g.dma_start(
                        out=shb[:, :], in_=qkvd.ap()[ttg * 128:(ttg + 1) * 128, :])],
                    waits=[ev_awr[ttg]] + prev_chunk)
                if ci > 0:
                    ev_cr = sd.dma("s", lambda g, ttg=ttg: g.dma_start(
                        out=carb[:, :], in_=qkvd.ap()[ttg * 128 - 3:ttg * 128, :]),
                        waits=[ev_awr[ttg]] + prev_chunk)
                else:
                    ev_cr = None

                def _cmm(t, ci=ci):
                    rr = None
                    for ti, po in ((0, 2048), (1, 2304), (2, 2560)):
                        rr = t.matmul(ps[:, po:po + 256], Bm[:, ti * 128:(ti + 1) * 128],
                                      shb[:, ti * 256:ti * 256 + 256], start=True, stop=(ci == 0))
                        if ci > 0:
                            rr = t.matmul(ps[:, po:po + 256], Bc[:, ti * 128:(ti + 1) * 128],
                                          carb[:, ti * 256:ti * 256 + 256], start=False, stop=True)
                    return rr
                ev_cmm = sd.step("t", _cmm, waits=[ev_ld, ev_cr, ev_B], signal=True)
                if upto == 'conv':
                    ev_fin = ev_cmm; break
                def _sl(s):
                    s.activation(cvo[:, 0:512], ps[:, 2048:2560], AF.Silu)
                    s.drain()
                    return s.activation(cvo[:, 512:768], ps[:, 2560:2816], AF.Silu)
                ev_sl = sd.step("s", _sl, waits=[ev_cmm], signal=True)
                def _sq(s, hl=hl):
                    s.activation(sqs[:, 0:128], cvo[:, hl * 128:hl * 128 + 128],
                                 AF.Square, accum_out=rqk[:, 0:1])
                    s.drain()
                    return s.activation(sqs[:, 128:256], cvo[:, 256 + hl * 128:256 + hl * 128 + 128],
                                        AF.Square, accum_out=rqk[:, 1:2])
                ev_sq = sd.step("s", _sq, waits=[ev_sl], signal=True)
                def _rs(s):
                    s.activation(rqk[:, 2:3], rqk[:, 0:1], AF.Ln, bias=px[:, 1:2], scale=px[:, 2:3])
                    s.drain()
                    s.activation(rqk[:, 3:4], rqk[:, 1:2], AF.Ln, bias=px[:, 0:1])
                    s.drain()
                    return s.activation(rqk[:, 2:4], rqk[:, 2:4], AF.Exp, scale=c["neg_half"][:, 0:1])
                ev_rs = sd.step("s", _rs, waits=[ev_sq], signal=True)
                if upto == 'silu':
                    ev_fin = ev_rs; break
                def _tm(v, hl=hl, bcol=bcol):
                    v.tensor_scalar(scn[:, SC["qs"]:SC["qs"] + 128], cvo[:, hl * 128:hl * 128 + 128],
                                    rqk[:, 2:3], None, op0=ALU.mult)
                    v.tensor_scalar(scn[:, SC["kh"]:SC["kh"] + 128],
                                    cvo[:, 256 + hl * 128:256 + hl * 128 + 128],
                                    rqk[:, 3:4], None, op0=ALU.mult)
                    v.tensor_single_scalar(nbc1[:, 0:1], bcol, -1.0, op=ALU.mult)
                    v.drain()
                    v.tensor_scalar(yf[:, 128:256], scn[:, SC["kh"]:SC["kh"] + 128],
                                    bcol, None, op0=ALU.mult)
                    v.tensor_scalar(yf[:, 0:128],
                                    cvo[:, 512 + hl * 128:512 + hl * 128 + 128],
                                    bcol, None, op0=ALU.mult)
                ev_tm = sd.step("v", _tm, waits=[ev_rs], signal=True)
                if upto == 'tm':
                    ev_fin = ev_tm; break
                ev_t1 = sd.step("t", lambda t: t.transpose(
                    psb[:, 5632:5760], scn[:, SC["qs"]:SC["qs"] + 128], c["ident"][:, :]),
                    waits=[ev_tm], signal=True)
                ev_e1 = sd.step("s", lambda s: s.copy(scn[:, SC["qsf"]:SC["qsf"] + 128],
                                                      psb[:, 5632:5760]), waits=[ev_t1], signal=True)
                ev_t2 = sd.step("t", lambda t: t.transpose(
                    psb[:, 5760:5888], scn[:, SC["kh"]:SC["kh"] + 128], c["ident"][:, :]),
                    waits=[ev_tm, ev_e1], signal=True)
                ev_e2 = sd.step("s", lambda s: s.copy(scn[:, SC["khf"]:SC["khf"] + 128],
                                                      psb[:, 5760:5888]), waits=[ev_t2], signal=True)
                if upto == 'tp':
                    ev_fin = ev_e2; break
                ev_g = sd.step("t", lambda t: t.matmul(
                    ps[:, 3072:3200], scn[:, SC["khf"]:SC["khf"] + 128],
                    scn[:, SC["khf"]:SC["khf"] + 128], start=True, stop=True),
                    waits=[ev_e2], signal=True)
                ev_t1v = sd.step("v", lambda v: v.tensor_scalar(
                    t1f[:, :], ps[:, 3072:3200], nbc1[:, 0:1], None, op0=ALU.mult),
                    waits=[ev_g, ev_tm], signal=True)
                ev_tpt1 = sd.step("t", lambda t: t.transpose(
                    ps[:, 3200:3328], t1f[:, :], c["ident_f"][:, :]),
                    waits=[ev_t1v], signal=True)
                ev_t1T = sd.step("s", lambda s: s.copy(t1T[:, :], ps[:, 3200:3328]),
                                 waits=[ev_tpt1], signal=True)
                def _bd(v):
                    v.tensor_tensor(BDf[:, :], t1f[:, :], mskf[:, 0:128], op=ALU.mult)
                    v.tensor_tensor(Z1f[:, :], t1T[:, :], mskf[:, 128:256], op=ALU.mult)
                    v.tensor_tensor(AoNT[:, :], t1T[:, :], mskf[:, 256:384], op=ALU.mult)
                    v.drain()
                    v.tensor_tensor(Pf[:, :], c["ident_f"][:, :], BDf[:, :], op=ALU.add)
                ev_bd = sd.step("v", _bd, waits=[ev_t1T], signal=True)
                if upto == 'lu':
                    ev_fin = ev_bd; break
                lcur, zcur = BDf[:, :], Z1f[:, :]
                prev_ev = ev_bd
                prev_ch = ev_bd
                for lvl in range(4):
                    slot = lvl % 2
                    lnew = Lc[:, slot * 128:slot * 128 + 128]
                    znew = Zc[:, slot * 128:slot * 128 + 128]
                    def _sq2(t, lc=lcur, zc=zcur, last=(lvl == 3)):
                        rr = None
                        if not last:
                            rr = t.matmul(ps[:, 3328:3456], zc, lc, start=True, stop=True)
                        return t.matmul(ps[:, 3456:3584], lc, zc, start=True, stop=True) or rr
                    ev_sq2 = sd.step("t", _sq2, waits=[prev_ev, prev_ch], signal=True)
                    def _sqe(s, ln=lnew, zn=znew, last=(lvl == 3)):
                        if not last:
                            s.copy(ln, ps[:, 3328:3456])
                            s.drain()
                        return s.copy(zn, ps[:, 3456:3584])
                    prev_ev = sd.step("s", _sqe, waits=[ev_sq2], signal=True)
                    def _chain(t, zn=znew):
                        t.matmul(ps[:, 3584:3712], zn, Pf[:, :], start=True, stop=False)
                        return t.matmul(ps[:, 3584:3712], c["ident_f"][:, :], Pf[:, :],
                                        start=False, stop=True)
                    ev_chm = sd.step("t", _chain, waits=[prev_ev, prev_ch], signal=True)
                    prev_ch = sd.step("s", lambda s: s.copy(Pf[:, :], ps[:, 3584:3712]),
                                      waits=[ev_chm], signal=True)
                    lcur, zcur = lnew, znew
                ev_tt = prev_ch
                if upto == 'sq':
                    ev_fin = ev_tt; break
                ev_tdm = sd.step("t", lambda t: t.transpose(
                    ps[:, 3712:3840], Pf[:, :], c["ident_f"][:, :]), waits=[ev_tt], signal=True)
                ev_tdt = sd.step("s", lambda s: s.copy(TdT[:, :], ps[:, 3712:3840]),
                                 waits=[ev_tdm], signal=True)
                if upto == 'tdt':
                    ev_fin = ev_tdt; break
                ev_itm = sd.step("t", lambda t: t.matmul(
                    ps[:, 3584:3840], TdT[:, :], yf[:, :], start=True, stop=True),
                    waits=[ev_tdt, ev_tm], signal=True)
                ev_ite = sd.step("s", lambda s: s.copy(unf[:, :], ps[:, 3584:3840]),
                                 waits=[ev_itm], signal=True)
                if upto == 'it0':
                    ev_fin = ev_ite; break
                for it in range(3):
                    def _ra(t):
                        t.matmul(ps[:, 3840:4096], c["ident_f"][:, :], yf[:, :],
                                 start=True, stop=False)
                        return t.matmul(ps[:, 3840:4096], AoNT[:, :], unf[:, :],
                                        start=False, stop=True)
                    ev_ram = sd.step("t", _ra, waits=[ev_ite, ev_bd], signal=True)
                    ev_rae = sd.step("s", lambda s: s.copy(rtf[:, :], ps[:, 3840:4096]),
                                     waits=[ev_ram], signal=True)
                    ev_itm = sd.step("t", lambda t: t.matmul(
                        ps[:, 3584:3840], TdT[:, :], rtf[:, :], start=True, stop=True),
                        waits=[ev_rae], signal=True)
                    if it < 2:
                        ev_ite = sd.step("s", lambda s: s.copy(unf[:, :], ps[:, 3584:3840]),
                                         waits=[ev_itm], signal=True)
                def _uw(s):
                    s.copy(u0f[:, :], ps[:, 3584:3712])
                    s.drain()
                    return s.copy(scn[:, SC["wtm"]:SC["wtm"] + 128], ps[:, 3712:3840])
                ev_uw = sd.step("s", _uw, waits=[ev_itm], signal=True)
                if upto == 'uw':
                    ev_fin = ev_uw; break
                ev_twm = sd.step("t", lambda t: t.transpose(
                    psb[:, 6016:6144], scn[:, SC["wtm"]:SC["wtm"] + 128], c["ident"][:, :]),
                    waits=[ev_uw], signal=True)
                ev_wf = sd.step("s", lambda s: s.copy(scn[:, SC["wfm"]:SC["wfm"] + 128],
                                                      psb[:, 6016:6144]), waits=[ev_twm], signal=True)
                ev_u0 = ev_uw
                if upto == 'wf':
                    ev_fin = ev_wf; break
                if upto == 'u0wf':
                    ev_fin = ev_wf; break
                ev_wsm = sd.step("t", lambda t, pi=pi: t.matmul(
                    ps[:, 3328:3456], scn[:, SC["wfm"]:SC["wfm"] + 128],
                    Sbf[:, pi * 128:(pi + 1) * 128], start=True, stop=True),
                    waits=[ev_wf] + prev_chunk, signal=True)
                ev_u = sd.step("v", lambda v: v.tensor_tensor(
                    scn[:, SC["u"]:SC["u"] + 128], u0f[:, :],
                    ps[:, 3328:3456], op=ALU.subtract), waits=[ev_wsm, ev_u0], signal=True)
                if upto == 'ws':
                    ev_fin = ev_u; break
                ev_atm = sd.step("t", lambda t: t.matmul(
                    ps[:, 3712:3840], scn[:, SC["khf"]:SC["khf"] + 128],
                    scn[:, SC["qsf"]:SC["qsf"] + 128], start=True, stop=True),
                    waits=[ev_e1, ev_e2, ev_uw], signal=True)
                if upto == 'atm':
                    ev_fin = ev_atm; break
                ev_atn = sd.step("v", lambda v: v.tensor_tensor(
                    scn[:, SC["attn"]:SC["attn"] + 128], ps[:, 3712:3840],
                    mskf[:, 384:512], op=ALU.mult), waits=[ev_atm], signal=True)
                if upto == 'at':
                    ev_fin = ev_atn; break
                def _o(t, pi=pi):
                    t.matmul(ps[:, 3712:3840], scn[:, SC["qsf"]:SC["qsf"] + 128],
                             Sbf[:, pi * 128:(pi + 1) * 128], start=True, stop=False)
                    return t.matmul(ps[:, 3712:3840], scn[:, SC["attn"]:SC["attn"] + 128],
                                    scn[:, SC["u"]:SC["u"] + 128], start=False, stop=True)
                ev_om = sd.step("t", _o, waits=[ev_atn, ev_u] + prev_chunk, signal=True)
                if upto == 'om':
                    ev_fin = ev_om; break
                ev_dsm = sd.step("t", lambda t: t.matmul(
                    ps[:, 3328:3456], scn[:, SC["kh"]:SC["kh"] + 128],
                    scn[:, SC["u"]:SC["u"] + 128], start=True, stop=True),
                    waits=[ev_u, ev_u0], signal=True)
                if upto == 'ds':
                    ev_fin = ev_dsm; break
                def _sup(v, pi=pi, ci=ci):
                    so = Ssb[:, pi * 256 + (ci % 2) * 128:pi * 256 + (ci % 2) * 128 + 128]
                    sn = Ssb[:, pi * 256 + ((ci + 1) % 2) * 128:pi * 256 + ((ci + 1) % 2) * 128 + 128]
                    v.tensor_tensor(sn, so, ps[:, 3328:3456], op=ALU.add)
                    v.drain()
                    v.tensor_copy(Sbf[:, pi * 128:(pi + 1) * 128], sn)
                ev_sup = sd.step("v", _sup, waits=[ev_dsm, ev_om], signal=True)
                if upto == 'scan':
                    ev_fin = ev_sup; break
                def _orm(s):
                    s.copy(otmp[:, :], ps[:, 3712:3840])
                    s.drain()
                    s.activation(sqs[:, 0:128], otmp[:, :], AF.Square, accum_out=rmsc[:, 0:1])
                    s.drain()
                    s.activation(rmsc[:, 1:2], rmsc[:, 0:1], AF.Ln, bias=px[:, 4:5], scale=px[:, 3:4])
                    s.drain()
                    return s.activation(rmsc[:, 1:2], rmsc[:, 1:2], AF.Exp, scale=c["neg_half"][:, 0:1])
                ev_orm = sd.step("s", _orm, waits=[ev_om], signal=True)
                ev_on = sd.step("v", lambda v: v.scalar_tensor_tensor(
                    osb[:, :], otmp[:, :], rmsc[:, 1:2], wbc[:, :],
                    op0=ALU.mult, op1=ALU.mult), waits=[ev_orm, ev_wbc], signal=True)
                if upto == 'orm':
                    ev_fin = ev_on; break
                ev_to = sd.step("t", lambda t: t.transpose(
                    psb[:, 5888:6016], osb[:, :], c["ident"][:, :]),
                    waits=[ev_on, ev_ow_hist[-2] if len(ev_ow_hist) >= 2 else None], signal=True)
                ev_oe = sd.step("s", lambda s, ci=ci: s.copy(
                    ost[:, (ci % 2) * 128:(ci % 2) * 128 + 128], psb[:, 5888:6016]),
                    waits=[ev_to], signal=True)
                ev_ow = sd.dma("y", lambda y, b=b, hl=hl, ci=ci: y.dma_start(
                    out=ofm[hl * 128:(hl + 1) * 128, b * S + ci * 128:b * S + ci * 128 + 128],
                    in_=ost[:, (ci % 2) * 128:(ci % 2) * 128 + 128]), waits=[ev_oe])
                ev_ow_hist.append(ev_ow)
                prev_chunk = [ev_sup, ev_oe]
        sd.step("y", lambda y: y.nop(),
                waits=[ev_fin if ev_fin is not None else
                       (ev_ow_hist[-1] if ev_ow_hist else ev_awr[-1])])
        sd.run(block)
    return nc
